# version 07
# baseline (speedup 1.0000x reference)
import numpy as np
import ml_dtypes
import concourse.bass as bass
import concourse.mybir as mybir
from concourse.bass_utils import run_bass_kernel_spmd

F32 = mybir.dt.float32
BF16 = mybir.dt.bfloat16
AF = mybir.ActivationFunctionType
ALU = mybir.AluOpType
AX = mybir.AxisListType

L, D, NST, HID = 8192, 1024, 16, 4096
T = 512
NTC = T // 128
NDC = D // 128
NOC = HID // 128
EPS = 1e-5
RD = 1.0 / D

ENGINES = ("sync", "gpsimd", "tensor", "vector", "scalar")
_UNIT = {"sync": 16, "gpsimd": 16, "tensor": 1, "vector": 1, "scalar": 1}
_DMA_RING = {"sync": 6, "gpsimd": 3}


class Buf:
    __slots__ = ("name", "last_w", "readers")

    def __init__(self, name):
        self.name = name
        self.last_w = None
        self.readers = []


class Sched:
    def __init__(self):
        self.steps = []

    def add(self, eng, emit, reads=(), writes=(), note="", dma=None):
        if dma is None:
            dma = eng in ("sync", "gpsimd")
        sid = len(self.steps)
        deps = set()
        for b in reads:
            if b.last_w is not None:
                deps.add(b.last_w)
        for b in writes:
            if b.last_w is not None:
                deps.add(b.last_w)
            for r in b.readers:
                deps.add(r)
        self.steps.append(dict(id=sid, eng=eng, emit=emit, deps=deps, dma=dma,
                               note=note, needed=False, cum=0))
        for b in reads:
            b.readers.append(sid)
        for b in writes:
            b.last_w = sid
            b.readers = []
        return sid

    def finalize(self):
        steps = self.steps
        eng_of = [s["eng"] for s in steps]
        for s in steps:
            for d in s["deps"]:
                if eng_of[d] != s["eng"] or steps[d]["dma"]:
                    steps[d]["needed"] = True
        for s in steps:
            if s["dma"]:
                s["needed"] = True
        cum = {}
        eseq = {e: 0 for e in ENGINES}
        dma_ct = {e: 0 for e in _DMA_RING}
        for s in steps:
            if s["dma"]:
                j = dma_ct[s["eng"]]
                dma_ct[s["eng"]] += 1
                s["sem"] = s["eng"] + str(j % _DMA_RING[s["eng"]])
            else:
                s["sem"] = s["eng"]
            if s["needed"]:
                cum[s["sem"]] = cum.get(s["sem"], 0) + _UNIT[s["eng"]]
            s["cum"] = cum.get(s["sem"], 0)
            s["eseq"] = eseq[s["eng"]]
            eseq[s["eng"]] += 1
        self.totals = dict(cum)

    def emit_engine(self, ename, eng, sems):
        steps = self.steps
        guar = {}
        drains_ok = ename in ("vector", "scalar", "gpsimd")
        drained_upto = 0
        n = 0
        for s in steps:
            if s["eng"] != ename:
                continue
            need = {}
            same_need_drain = False
            for d in s["deps"]:
                de = steps[d]["eng"]
                if de == ename and not steps[d]["dma"]:
                    if (drains_ok and not s["dma"] and steps[d]["eseq"] >= drained_upto
                            and steps[d]["eseq"] >= s["eseq"] - 1):
                        same_need_drain = True
                    continue
                assert steps[d]["needed"], (s["note"], steps[d]["note"])
                sm = steps[d]["sem"]
                t = steps[d]["cum"]
                if t > guar.get(sm, 0) and t > need.get(sm, 0):
                    need[sm] = t
            if same_need_drain:
                eng.drain()
                drained_upto = s["eseq"]
            for sm in sorted(need):
                eng.wait_ge(sems[sm], need[sm])
                guar[sm] = need[sm]
            last = s["emit"](eng)
            if s["needed"]:
                assert last is not None, f"step {s['id']} {s['note']} returned no instruction"
                last.then_inc(sems[s["sem"]], _UNIT[ename])
            n += 1
        return n


def host_prep(inputs):
    g = lambda k: np.asarray(inputs[k], dtype=np.float32)

    def prep_mix(w):
        return np.ascontiguousarray(
            w.reshape(NDC, 128, NDC, 128).transpose(2, 1, 0, 3).reshape(NDC, 128, D))

    def prep_T(w):
        O, I = w.shape
        no, ni = O // 128, I // 128
        return np.ascontiguousarray(
            w.reshape(no, 128, ni, 128).transpose(0, 3, 2, 1).reshape(no, 128, I))

    def cols(v):
        return np.ascontiguousarray(np.asarray(v, np.float32).reshape(-1, 128).T)

    d = {}
    d["w1m"] = prep_mix(g("L0_attn_w1_w"))
    d["v1m"] = prep_mix(g("L0_attn_v1_w"))
    d["wdtm"] = prep_mix(g("L0_attn_W_dt"))
    d["w2m"] = prep_mix(g("L0_attn_w2_w"))
    d["w1f"] = prep_T(g("L0_ffn_w1_w"))
    d["w2f"] = prep_T(g("L0_ffn_w2_w"))
    d["h1f"] = prep_T(g("L0_ffn_h1_w"))
    d["h2f"] = prep_T(g("L0_ffn_h2_w"))
    d["w3f"] = prep_T(g("L0_ffn_w3_w"))
    d["alog"] = np.ascontiguousarray(
        g("L0_attn_A_log").reshape(NDC, 128, NST).transpose(1, 0, 2).reshape(128, NDC * NST))
    d["convw"] = np.ascontiguousarray(
        g("L0_attn_conv_w").reshape(D, 3).reshape(NDC, 128, 3).transpose(1, 0, 2).reshape(128, NDC * 3))
    d["dsk"] = cols(g("L0_attn_D_skip"))
    d["w1b"] = cols(g("L0_attn_w1_b"))
    d["v1b"] = cols(g("L0_attn_v1_b"))
    d["w2b"] = cols(g("L0_attn_w2_b"))
    d["bdt"] = cols(g("L0_attn_b_dt"))
    d["w1s"] = cols(g("L0_ffn_w1_s"))
    d["w2s"] = cols(g("L0_ffn_w2_s"))
    d["h1s"] = cols(g("L0_ffn_h1_s"))
    d["h2s"] = cols(g("L0_ffn_h2_s"))
    d["w3s"] = cols(g("L0_ffn_w3_s"))
    d["wb"] = np.ascontiguousarray(
        g("L0_attn_W_B").reshape(NDC, 128, NST).transpose(1, 0, 2).reshape(128, NDC * NST))
    d["wc"] = np.ascontiguousarray(
        g("L0_attn_W_C").reshape(NDC, 128, NST).transpose(1, 0, 2).reshape(128, NDC * NST))
    d["gcol"] = cols(g("L0_attn_ln_g"))
    d["bcol"] = cols(g("L0_attn_ln_b"))
    d["onesf"] = np.ones((128, 1), np.float32)
    d["zerof"] = np.zeros((128, 1), np.float32)
    d["epsf"] = np.full((128, 1), 1e-5, np.float32)
    d["onecol"] = np.ones((1, 1), ml_dtypes.bfloat16)
    d["idb"] = np.eye(128, dtype=ml_dtypes.bfloat16)
    return d


def build_program(n_tiles=L // T, debug_taps=(), stage=99):
    nc = bass.Bass("TRN2", target_bir_lowering=False)

    x_d = nc.dram_tensor("x", [L, D], F32, kind="ExternalInput").ap()
    out_d = nc.dram_tensor("out", [L, D], F32, kind="ExternalOutput").ap()

    wspec = dict(w1m=(NDC, 128, D), v1m=(NDC, 128, D), wdtm=(NDC, 128, D),
                 w2m=(NDC, 128, D), w1f=(NOC, 128, D), w2f=(NOC, 128, D),
                 h1f=(NOC, 128, HID), h2f=(NOC, 128, HID), w3f=(NDC, 128, HID))
    wd = {}
    ws = {}
    for k, shp in wspec.items():
        wd[k] = nc.dram_tensor(k, list(shp), F32, kind="ExternalInput").ap()
        ws[k] = nc.dram_tensor(k + "S", list(shp), BF16, kind="Internal").ap()

    cspec_f32 = dict(alog=(128, NDC * NST), convw=(128, NDC * 3), dsk=(128, NDC),
                     w1b=(128, NDC), v1b=(128, NDC), w2b=(128, NDC), bdt=(128, NDC),
                     w1s=(128, NOC), w2s=(128, NOC), h1s=(128, NOC), h2s=(128, NOC),
                     w3s=(128, NDC), gcol=(128, NDC), bcol=(128, NDC))
    cd = {k: nc.dram_tensor(k, list(s), F32, kind="ExternalInput").ap()
          for k, s in cspec_f32.items()}
    cd["wb"] = nc.dram_tensor("wb", [128, NDC * NST], F32, kind="ExternalInput").ap()
    cd["wc"] = nc.dram_tensor("wc", [128, NDC * NST], F32, kind="ExternalInput").ap()

    cd["onecol"] = nc.dram_tensor("onecol", [1, 1], BF16, kind="ExternalInput").ap()
    cd["idb"] = nc.dram_tensor("idb", [128, 128], BF16, kind="ExternalInput").ap()
    cd["onesf"] = nc.dram_tensor("onesf", [128, 1], F32, kind="ExternalInput").ap()
    cd["zerof"] = nc.dram_tensor("zerof", [128, 1], F32, kind="ExternalInput").ap()
    cd["epsf"] = nc.dram_tensor("epsf", [128, 1], F32, kind="ExternalInput").ap()

    ctx = []

    def sb(name, shape, dt):
        h = nc.sbuf_tensor(name, list(shape), dt)
        ctx.append(h)
        return h.__enter__()

    def psum(name, shape, dt):
        h = nc.psum_tensor(name, list(shape), dt)
        ctx.append(h)
        return h.__enter__()

    xfr = [sb(f"xfr{j}", [128, D], F32) for j in range(2)]
    xtl = [sb(f"xtl{j}", [128, D], F32) for j in range(2)]
    sq = sb("sq", [128, D], F32)
    xn_tm = sb("xn_tm", [128, D], BF16)
    xrm_tm = sq
    xm_tm = xn_tm
    stats = sb("stats", [128, 32], F32)
    xn_fm = sb("xn_fm", [128, NDC * T], BF16)
    xm_fm = sb("xm_fm", [128, NDC * T], BF16)
    x1 = sb("x1", [128, NDC * (T + 2)], BF16)
    u_bf = sb("u_bf", [128, NDC * T], BF16)
    dt_bf = sb("dt_bf", [128, NDC * T], BF16)
    w_all = sb("w_all", [128, NDC * T], BF16)
    y_f = sb("y_f", [128, NDC * T], F32)
    attn = sb("attn", [128, NDC * T], BF16)
    hA = sb("hA", [128, NOC * T], BF16)
    hB = sb("hB", [128, NOC * T], BF16)
    aS = [sb(f"aS{j}", [128, T], F32) for j in range(4)]
    bS = [sb(f"bS{j}", [128, T], F32) for j in range(2)]
    hS = [sb(f"hS{j}", [128, T], F32) for j in range(2)]
    hstate = sb("hstate", [128, NDC * NST], F32)
    x1b = sb("x1b", [128, NDC], BF16)
    xcol = sb("xcol", [128, NDC], BF16)
    Bt_sb = sb("Bt_sb", [16, T], BF16)
    Ct_sb = sb("Ct_sb", [16, T], BF16)
    a_scr = [sb(f"a_scr{j}", [128, T], BF16) for j in range(2)]
    wffn = [sb(f"wffn{j}", [128, HID], BF16) for j in range(2)]
    wmix = wffn
    out_sb = [sb(f"out_sb{j}", [128, D], F32) for j in range(2)]
    s_scr = [sb("s_scr0", [128, T], F32)]
    x_ex = out_sb[1][0:1, :]
    xn_ex = x_ex
    xm_ex = xm_tm
    ex_st = sb("ex_st", [1, 16], F32)
    WBb = sb("WBb", [128, NDC * NST], BF16)
    WCb = sb("WCb", [128, NDC * NST], BF16)
    csb = {}
    for k, s in cspec_f32.items():
        csb[k] = sb("c_" + k, list(s), F32)
    csb["onecol"] = sb("c_onecol", [1, 1], BF16)
    csb["idb"] = sb("c_idb", [128, 128], BF16)
    csb["onesf"] = sb("c_onesf", [128, 1], F32)
    csb["zerof"] = sb("c_zerof", [128, 1], F32)
    csb["epsf"] = sb("c_epsf", [128, 1], F32)

    ps_mm = [psum(f"ps_mm{j}", [128, T], F32) for j in range(4)]
    ps_bcB = psum("ps_bcB", [128, T], F32)
    ps_bcC = psum("ps_bcC", [128, T], F32)
    ps_tail = psum("ps_tail", [128, D], BF16)

    B = {}

    def bufof(name):
        if name not in B:
            B[name] = Buf(name)
        return B[name]

    bufof("xcol")
    for nm in ("sq xn_tm xm_tm stats xn_fm x1b Bt Ct x_ex xm_ex ex_st "
               "WBb WCb consts psB psC psT xfr0 xfr1 xtl0 xtl1").split():
        bufof(nm)

    def R(base, n):
        return [bufof(f"{base}{i}") for i in range(n)]

    def R1(base, i):
        return bufof(f"{base}{i}")
    RY, RDT, RW, RU, RX1, RXM, RAT, RHS = (R("y", NDC), R("dt", NDC), R("wal", NDC),
                                           R("u", NDC), R("x1r", NDC), R("xmf", NDC),
                                           R("at", NDC), R("hst", NDC))
    RHA, RHB = R("hA", NOC), R("hB", NOC)
    for j in range(2):
        for nm in ("a_scr", "out_sb", "wffn", "bS", "hS"):
            bufof(f"{nm}{j}")
    for j in range(4):
        bufof(f"aS{j}")
    bufof("s_scr0")
    for j in range(4):
        bufof(f"ps{j}")
    for k in wspec:
        bufof("scr_" + k)

    S = Sched()

    def c_load(k):
        S.add("sync", lambda e, k=k: e.dma_start(out=csb[k][:, :], in_=cd[k][:, :]),
              writes=[bufof("consts")], note=f"load {k}")
    for k in list(cspec_f32) + ["onecol", "idb", "onesf", "zerof", "epsf"]:
        c_load(k)
    for k in wspec:
        S.add("gpsimd", lambda e, k=k: e.dma_start(out=ws[k][:, :, :], in_=wd[k][:, :, :]),
              writes=[bufof("scr_" + k)], note=f"cast {k}")
    A_sb = csb["alog"]
    bufof("A")
    S.add("scalar", lambda e: e.activation(out=A_sb[:, :], in_=csb["alog"][:, :], func=AF.Exp),
          reads=[bufof("consts")], writes=[bufof("A")], note="expA")
    S.add("vector", lambda e: e.tensor_scalar_mul(A_sb[:, :], A_sb[:, :], -1.0),
          reads=[bufof("A")], writes=[bufof("A")], note="negA")
    S.add("gpsimd", lambda e: e.dma_start(out=WBb[:, :], in_=cd["wb"][:, :]),
          writes=[bufof("WBb")], note="WBb")
    S.add("gpsimd", lambda e: e.dma_start(out=WCb[:, :], in_=cd["wc"][:, :]),
          writes=[bufof("WCb")], note="WCb")
    S.add("vector", lambda e: e.memset(hstate[:, :], 0.0), writes=[bufof("hstate")], note="hs0")
    S.add("vector", lambda e: e.memset(x1b[:, :], 0.0), writes=[bufof("x1b")], note="x1b0")

    def fm(bufh, dc, lo=0, hi=T):
        return bufh[:, dc * T + lo: dc * T + hi]

    ACC = dict(mult=ALU.mult, add=ALU.add, subtract=ALU.subtract)

    def emit_front(i):
        tb = f"t{i}."
        if i + 1 < L // T:
            S.add("gpsimd",
                  lambda e, i=i: e.dma_start(out=x_ex, in_=x_d[(i + 1) * T:(i + 1) * T + 1, :]),
                  writes=[bufof("out_sb1")], note=tb + "xex")

        for tc in range(NTC):
            S.add("gpsimd",
                  lambda e, i=i, tc=tc: e.dma_start(
                      out=xfr[tc % 2][:, :],
                      in_=x_d[i * T + tc * 128: i * T + (tc + 1) * 128, :]),
                  reads=[], writes=[bufof(f"xfr{tc % 2}")], note=tb + f"xload{tc}")
            xch = xfr[tc % 2][:, :]
            xbuf = bufof(f"xfr{tc % 2}")
            st = stats
            c0 = 0
            S.add("vector", lambda e, xch=xch: e.reduce_sum(st[:, 0:1], xch, axis=AX.X),
                  reads=[xbuf], writes=[bufof("stats")], note=tb + "s1")
            S.add("scalar", lambda e, xch=xch: e.activation(out=sq[:, :], in_=xch, func=AF.Square),
                  reads=[xbuf], writes=[bufof("sq")], note=tb + "sq1")
            S.add("vector", lambda e: e.reduce_sum(st[:, 1:2], sq[:, :], axis=AX.X),
                  reads=[bufof("sq")], writes=[bufof("stats")], note=tb + "q1")

            def ln_tail_steps(s, a, b, m, t, v, sbuf, nt):
                S.add("vector", lambda e: e.tensor_scalar_mul(s[:, m:m + 1], s[:, a:a + 1], RD),
                      reads=[sbuf], writes=[sbuf], note=nt + "mu")
                S.add("vector", lambda e: e.tensor_mul(s[:, t:t + 1], s[:, m:m + 1], s[:, m:m + 1]),
                      reads=[sbuf], writes=[sbuf], note=nt + "musq")
                S.add("vector", lambda e: e.scalar_tensor_tensor(s[:, v:v + 1], s[:, b:b + 1], RD,
                                                                 s[:, t:t + 1], ALU.mult, ALU.subtract),
                      reads=[sbuf], writes=[sbuf], note=nt + "var")
            ln_tail_steps(st, 0, 1, 2, 3, 4, bufof("stats"), tb + "v1.")

            def rstd_steps(s, v, t, r, sbuf, nt):
                npart = s.shape[0]
                S.add("scalar", lambda e: e.activation(out=s[:, t:t + 1], in_=s[:, v:v + 1], func=AF.Ln,
                                                       bias=csb["epsf"][0:npart, 0:1],
                                                       scale=csb["onesf"][0:npart, 0:1]),
                      reads=[sbuf], writes=[sbuf], note=nt + "ln")
                S.add("scalar", lambda e: e.activation(out=s[:, r:r + 1], in_=s[:, t:t + 1],
                                                       func=AF.Exp, scale=-0.5),
                      reads=[sbuf], writes=[sbuf], note=nt + "exp")
            rstd_steps(st, 4, 3, 5, bufof("stats"), tb + "r1.")
            S.add("vector", lambda e, xch=xch, s=st: e.tensor_scalar(
                      xn_tm[:, :], xch, s[:, 2:3], s[:, 5:6], ALU.subtract, ALU.mult),
                  reads=[xbuf, bufof("stats")], writes=[bufof("xn_tm")], note=tb + "xn")
            S.add("vector", lambda e: e.reduce_sum(st[:, 8:9], xn_tm[:, :], axis=AX.X),
                  reads=[bufof("xn_tm")], writes=[bufof("stats")], note=tb + "s2")
            S.add("scalar", lambda e: e.activation(out=sq[:, :], in_=xn_tm[:, :], func=AF.Square),
                  reads=[bufof("xn_tm")], writes=[bufof("sq")], note=tb + "sq2")
            S.add("vector", lambda e: e.reduce_sum(st[:, 9:10], sq[:, :], axis=AX.X),
                  reads=[bufof("sq")], writes=[bufof("stats")], note=tb + "q2")
            ln_tail_steps(st, 8, 9, 10, 11, 12, bufof("stats"), tb + "v2.")
            rstd_steps(st, 12, 11, 13, bufof("stats"), tb + "r2.")
            def tr(e, src):
                last = None
                for dc in range(NDC):
                    last = e.transpose(ps_tail[:, dc * 128:(dc + 1) * 128],
                                       src[:, dc * 128:(dc + 1) * 128], csb["idb"][:, :])
                return last
            def ev(e, dst, tcv):
                src3 = ps_tail[:, :].rearrange("p (dc t) -> p dc t", dc=NDC)
                dst3 = dst[:, :].rearrange("p (dc t) -> p dc t", dc=NDC)[:, :, tcv * 128:(tcv + 1) * 128]
                return e.copy(dst3, src3)
            S.add("tensor", lambda e, tr=tr: tr(e, xn_tm),
                  reads=[bufof("xn_tm"), bufof("consts")], writes=[bufof("psT")], note=tb + "trxn")
            S.add("scalar", lambda e, ev=ev, tc=tc: ev(e, xn_fm, tc),
                  reads=[bufof("psT")], writes=[bufof("xn_fm")], note=tb + "evxn")
            S.add("vector", lambda e, s=st: e.tensor_scalar(
                      xm_tm[:, :], xn_tm[:, :], s[:, 10:11], s[:, 13:14],
                      ALU.subtract, ALU.mult),
                  reads=[bufof("xn_tm"), bufof("stats")],
                  writes=[bufof("xn_tm")], note=tb + "xm1")
            S.add("tensor", lambda e, tr=tr: tr(e, xm_tm),
                  reads=[bufof("xn_tm"), bufof("consts")], writes=[bufof("psT")], note=tb + "trxm")
            def evxm(e, tcv):
                src3 = ps_tail[:, :].rearrange("p (dc t) -> p dc t", dc=NDC)
                last = None
                for dc in range(NDC):
                    last = e.tensor_scalar(
                        xm_fm[:, dc * T + tcv * 128: dc * T + (tcv + 1) * 128],
                        src3[:, dc, :], csb["gcol"][:, dc:dc + 1], csb["bcol"][:, dc:dc + 1],
                        ALU.mult, ALU.add)
                return last
            S.add("vector", lambda e, evxm=evxm, tc=tc: evxm(e, tc),
                  reads=[bufof("psT"), bufof("consts")], writes=RXM, note=tb + "evxm")

        if i + 1 < L // T:
            es = ex_st
            S.add("scalar", lambda e: e.activation(out=sq[0:1, :], in_=x_ex, func=AF.Square),
                  reads=[bufof("out_sb1")], writes=[bufof("sq")], note=tb + "exsq")
            S.add("vector", lambda e: e.reduce_sum(es[:, 0:1], x_ex, axis=AX.X),
                  reads=[bufof("out_sb1")], writes=[bufof("ex_st")], note=tb + "exsum")
            S.add("vector", lambda e: e.reduce_sum(es[:, 1:2], sq[0:1, :], axis=AX.X),
                  reads=[bufof("sq")], writes=[bufof("ex_st")], note=tb + "exsumq")
            ln_tail_steps(es, 0, 1, 2, 3, 4, bufof("ex_st"), tb + "exv1.")
            rstd_steps(es, 4, 3, 5, bufof("ex_st"), tb + "exr1.")
            S.add("vector", lambda e: e.tensor_scalar(
                      xn_ex, x_ex, es[:, 2:3], es[:, 5:6], ALU.subtract, ALU.mult),
                  reads=[bufof("out_sb1"), bufof("ex_st")], writes=[bufof("out_sb1")], note=tb + "exxn")
            S.add("vector", lambda e: e.reduce_sum(es[:, 8:9], xn_ex, axis=AX.X),
                  reads=[bufof("out_sb1")], writes=[bufof("ex_st")], note=tb + "exs2")
            S.add("scalar", lambda e: e.activation(out=sq[0:1, :], in_=xn_ex, func=AF.Square),
                  reads=[bufof("out_sb1")], writes=[bufof("sq")], note=tb + "exsq2")
            S.add("vector", lambda e: e.reduce_sum(es[:, 9:10], sq[0:1, :], axis=AX.X),
                  reads=[bufof("sq")], writes=[bufof("ex_st")], note=tb + "exq2")
            ln_tail_steps(es, 8, 9, 10, 11, 12, bufof("ex_st"), tb + "exv2.")
            rstd_steps(es, 12, 11, 13, bufof("ex_st"), tb + "exr2.")
            S.add("vector", lambda e: e.tensor_scalar(
                      xn_ex, xn_ex, es[:, 10:11], es[:, 13:14], ALU.subtract, ALU.mult),
                  reads=[bufof("out_sb1"), bufof("ex_st")],
                  writes=[bufof("out_sb1")], note=tb + "exxm1")
            S.add("vector", lambda e: e.tensor_copy(xm_ex[0:1, :], xn_ex),
                  reads=[bufof("out_sb1")], writes=[bufof("xn_tm")], note=tb + "exxm2")
            def exmm(e):
                last = None
                for dc in range(NDC):
                    last = e.matmul(ps_bcC[:, dc:dc + 1], xm_ex[0:1, dc * 128:(dc + 1) * 128],
                                    csb["onecol"][0:1, 0:1], start=True, stop=True)
                return last
            S.add("tensor", exmm, reads=[bufof("xn_tm"), bufof("consts")],
                  writes=[bufof("psC")], note=tb + "exmm")
            def excol(e):
                last = None
                for dc in range(NDC):
                    last = e.tensor_scalar(xcol[:, dc:dc + 1], ps_bcC[:, dc:dc + 1],
                                           csb["gcol"][:, dc:dc + 1], csb["bcol"][:, dc:dc + 1],
                                           ALU.mult, ALU.add)
                return last
            S.add("vector", excol,
                  reads=[bufof("psC"), bufof("consts")], writes=[bufof("xcol")], note=tb + "excol")


    for i in range(n_tiles):
        tb = f"t{i}."
        if i == 0:
            emit_front(0)
        def wslab(key, idx, bufset, j, nslab_note):
            S.add("sync",
                  lambda e, key=key, idx=idx, j=j: e.dma_start(out=wffn[j][:, 0:ws[key].shape[2]],
                                                               in_=ws[key][idx, :, :]),
                  reads=[bufof("scr_" + key)], writes=[bufof(f"wffn{j}")],
                  note=nslab_note)

        def wslab4(key, g, j, nslab_note):
            S.add("sync",
                  lambda e, key=key, g=g, j=j: e.dma_start(
                      out=wffn[j][:, :].rearrange("p (s f) -> p s f", s=4),
                      in_=ws[key][4 * g:4 * g + 4, :, :].rearrange("s p f -> p s f")),
                  reads=[bufof("scr_" + key)], writes=[bufof(f"wffn{j}")],
                  note=nslab_note)

        def mm_group(e, j, soff, pj, src, nk=NDC):
            last = None
            for k in range(nk):
                last = e.matmul(ps_mm[pj][:, :], wffn[j][:, soff + k * 128: soff + (k + 1) * 128],
                                fm(src, k), start=(k == 0), stop=(k == nk - 1))
            return last

        has_ex = stage >= 1 and i + 1 < L // T
        for g in range(2 if stage >= 1 else 0):
            j = g % 2
            wslab4("w1m", g, j, tb + f"w1m{g}")
            for s4 in range(4):
                oc = 4 * g + s4
                pj = oc % 4
                S.add("tensor", lambda e, j=j, s4=s4, pj=pj: mm_group(e, j, s4 * D, pj, xm_fm),
                      reads=[bufof(f"wffn{j}")] + RXM,
                      writes=[bufof(f"ps{pj}")], note=tb + f"x1mm{oc}")
                if has_ex:
                    def exg(e, j=j, s4=s4, oc=oc):
                        last = None
                        for k in range(NDC):
                            last = e.matmul(ps_bcB[:, oc:oc + 1],
                                            wffn[j][:, s4 * D + k * 128: s4 * D + (k + 1) * 128],
                                            xcol[:, k:k + 1], start=(k == 0), stop=(k == NDC - 1))
                        return last
                    S.add("tensor", exg, reads=[bufof(f"wffn{j}"), bufof("xcol")],
                          writes=[bufof("psB")], note=tb + f"x1exmm{oc}")
                S.add("scalar", lambda e, pj=pj, oc=oc: e.add(
                          x1[:, oc * (T + 2) + 1: oc * (T + 2) + 1 + T], ps_mm[pj][:, :],
                          csb["w1b"][:, oc:oc + 1]),
                      reads=[bufof(f"ps{pj}"), bufof("consts")], writes=[R1("x1r", oc)],
                      note=tb + f"x1ev{oc}")
        if stage >= 1 and i + 1 < L // T:
            def exev(e):
                last = None
                for oc in range(NDC):
                    base = oc * (T + 2)
                    last = e.add(x1[:, base + T + 1:base + T + 2],
                                 ps_bcB[:, oc:oc + 1], csb["w1b"][:, oc:oc + 1])
                return last
            S.add("scalar", exev,
                  reads=[bufof("psB"), bufof("consts")], writes=RX1, note=tb + "exev")
        elif stage >= 1:
            S.add("vector", lambda e: e.memset(
                      x1[:, :].rearrange("p (dc c) -> p dc c", dc=NDC)[:, :, T + 1:T + 2], 0.0),
                  writes=RX1, note=tb + "exz")
        if stage < 2:
            continue
        S.add("vector", lambda e: e.tensor_copy(
                  x1[:, :].rearrange("p (dc c) -> p dc c", dc=NDC)[:, :, 0:1],
                  x1b[:, :].rearrange("p (dc c) -> p dc c", c=1)),
              reads=[bufof("x1b")], writes=RX1, note=tb + "brest")
        for p in range(NDC // 2):
            for dc in (2 * p, 2 * p + 1):
                base = dc * (T + 2)
                S.add("vector", lambda e, base=base, dc=dc: e.tensor_scalar_mul(
                          bS[dc % 2][:, :], x1[:, base:base + T], csb["convw"][:, 3 * dc:3 * dc + 1]),
                      reads=[R1("x1r", dc), bufof("consts")], writes=[bufof(f"bS{dc % 2}")],
                      note=tb + f"conv{dc}a")
            for dc in (2 * p, 2 * p + 1):
                base = dc * (T + 2)
                S.add("vector", lambda e, base=base, dc=dc: e.scalar_tensor_tensor(
                          bS[dc % 2][:, :], x1[:, base + 1:base + 1 + T],
                          csb["convw"][:, 3 * dc + 1:3 * dc + 2], bS[dc % 2][:, :], ALU.mult, ALU.add),
                      reads=[R1("x1r", dc), bufof("consts"), bufof(f"bS{dc % 2}")],
                      writes=[bufof(f"bS{dc % 2}")], note=tb + f"conv{dc}b")
            for dc in (2 * p, 2 * p + 1):
                base = dc * (T + 2)
                S.add("vector", lambda e, base=base, dc=dc: e.scalar_tensor_tensor(
                          bS[dc % 2][:, :], x1[:, base + 2:base + 2 + T],
                          csb["convw"][:, 3 * dc + 2:3 * dc + 3], bS[dc % 2][:, :], ALU.mult, ALU.add),
                      reads=[R1("x1r", dc), bufof("consts"), bufof(f"bS{dc % 2}")],
                      writes=[bufof(f"bS{dc % 2}")], note=tb + f"conv{dc}c")
            for dc in (2 * p, 2 * p + 1):
                S.add("scalar", lambda e, dc=dc: e.activation(out=fm(u_bf, dc), in_=bS[dc % 2][:, :],
                                                              func=AF.Silu),
                      reads=[bufof(f"bS{dc % 2}")], writes=[R1("u", dc)], note=tb + f"usilu{dc}")
        S.add("vector", lambda e: e.tensor_copy(
                  x1b[:, :].rearrange("p (dc c) -> p dc c", c=1),
                  x1[:, :].rearrange("p (dc c) -> p dc c", dc=NDC)[:, :, T:T + 1]),
              reads=RX1, writes=[bufof("x1b")], note=tb + "bsave")

        if stage < 3:
            continue
        for oc in range(NDC):
            j = (oc // 4) % 2
            if oc % 4 == 0:
                wslab4("wdtm", oc // 4, j, tb + f"wdtm{oc}")
            pj = oc % 4
            S.add("tensor", lambda e, j=j, oc=oc, pj=pj: mm_group(e, j, (oc % 4) * D, pj, u_bf),
                  reads=[bufof(f"wffn{j}")] + RU, writes=[bufof(f"ps{pj}")],
                  note=tb + f"dtmm{oc}")
            S.add("scalar", lambda e, pj=pj, oc=oc: e.activation(
                      out=s_scr[0][:, :], in_=ps_mm[pj][:, :], func=AF.Abs,
                      bias=csb["bdt"][:, oc:oc + 1], scale=csb["onesf"][:, 0:1]),
                  reads=[bufof(f"ps{pj}"), bufof("consts")], writes=[bufof("s_scr0")],
                  note=tb + f"sp{oc}abs")
            S.add("scalar", lambda e: e.activation(
                      out=s_scr[0][:, :], in_=s_scr[0][:, :], func=AF.Exp, scale=-1.0),
                  reads=[bufof("s_scr0")], writes=[bufof("s_scr0")], note=tb + f"sp{oc}exp")
            S.add("scalar", lambda e: e.activation(
                      out=s_scr[0][:, :], in_=s_scr[0][:, :], func=AF.Ln, bias=1.0),
                  reads=[bufof("s_scr0")], writes=[bufof("s_scr0")], note=tb + f"sp{oc}ln")
            S.add("scalar", lambda e, pj=pj, oc=oc: e.activation(
                      out=aS[0][:, :], in_=ps_mm[pj][:, :], func=AF.Relu,
                      bias=csb["bdt"][:, oc:oc + 1], scale=csb["onesf"][:, 0:1]),
                  reads=[bufof(f"ps{pj}"), bufof("consts")], writes=[bufof("aS0")],
                  note=tb + f"sp{oc}relu")
            S.add("vector", lambda e, oc=oc: e.tensor_add(fm(dt_bf, oc), s_scr[0][:, :], aS[0][:, :]),
                  reads=[bufof("s_scr0"), bufof("aS0")], writes=[R1("dt", oc)],
                  note=tb + f"dtadd{oc}")

        if stage < 4:
            continue
        def btmm(e, wsb, ps):
            last = None
            for dc in range(NDC):
                last = e.matmul(ps[0:16, :], wsb[:, dc * NST:(dc + 1) * NST], fm(u_bf, dc),
                                start=(dc == 0), stop=(dc == NDC - 1))
            return last
        S.add("tensor", lambda e: btmm(e, WBb, ps_bcB),
              reads=[bufof("WBb")] + RU, writes=[bufof("psB")], note=tb + "btmm")
        S.add("scalar", lambda e: e.copy(Bt_sb[:, :], ps_bcB[0:16, :]),
              reads=[bufof("psB")], writes=[bufof("Bt")], note=tb + "btev")

        S.add("tensor", lambda e: btmm(e, WCb, ps_bcC),
              reads=[bufof("WCb")] + RU, writes=[bufof("psC")], note=tb + "ctmm")
        S.add("scalar", lambda e: e.copy(Ct_sb[:, :], ps_bcC[0:16, :]),
              reads=[bufof("psC")], writes=[bufof("Ct")], note=tb + "ctev")


        for oc in range(NDC):
            j = (oc // 4) % 2
            if oc % 4 == 0:
                wslab4("v1m", oc // 4, j, tb + f"v1m{oc}")
            pj = oc % 4
            S.add("tensor", lambda e, j=j, oc=oc, pj=pj: mm_group(e, j, (oc % 4) * D, pj, xm_fm),
                  reads=[bufof(f"wffn{j}")] + RXM, writes=[bufof(f"ps{pj}")],
                  note=tb + f"vmm{oc}")
            S.add("scalar", lambda e, pj=pj, oc=oc: e.activation(
                      out=x1[:, oc * (T + 2) + 1: oc * (T + 2) + 1 + T], in_=ps_mm[pj][:, :],
                      func=AF.Silu, bias=csb["v1b"][:, oc:oc + 1], scale=csb["onesf"][:, 0:1]),
                  reads=[bufof(f"ps{pj}"), bufof("consts")], writes=[R1("x1r", oc)],
                  note=tb + f"vev{oc}")

        for dc in range(NDC):
            S.add("vector", lambda e, dc=dc: e.tensor_mul(fm(w_all, dc), fm(dt_bf, dc), fm(u_bf, dc)),
                  reads=[R1("dt", dc), R1("u", dc)], writes=[R1("wal", dc)], note=tb + f"w{dc}")
        S.add("vector", lambda e: e.memset(y_f[:, :], 0.0), writes=RY, note=tb + "y0")

        if stage < 5:
            continue
        ffn_steps = []

        def add_w12(hoc):
            j = (hoc // 4) % 2
            jj = 1 - j
            pj = (2 * hoc) % 4
            pj2 = (2 * hoc + 1) % 4
            s4 = hoc % 4
            if s4 == 0:
                wslab4("w1f", hoc // 4, j, tb + f"w1f{hoc}")
                wslab4("w2f", hoc // 4, jj, tb + f"w2f{hoc}")
            S.add("tensor", lambda e, j=j, s4=s4, pj=pj: mm_group(e, j, s4 * D, pj, xn_fm),
                  reads=[bufof(f"wffn{j}"), bufof("xn_fm")], writes=[bufof(f"ps{pj}")],
                  note=tb + f"w1mm{hoc}")
            S.add("scalar", lambda e, pj=pj, hoc=hoc: e.activation(
                      out=a_scr[hoc % 2][:, :], in_=ps_mm[pj][:, :], func=AF.Silu,
                      scale=csb["w1s"][:, hoc:hoc + 1], bias=csb["zerof"][:, 0:1]),
                  reads=[bufof(f"ps{pj}"), bufof("consts")], writes=[bufof(f"a_scr{hoc % 2}")],
                  note=tb + f"w1ev{hoc}")
            S.add("tensor", lambda e, jj=jj, s4=s4, pj2=pj2: mm_group(e, jj, s4 * D, pj2, xn_fm),
                  reads=[bufof(f"wffn{jj}"), bufof("xn_fm")], writes=[bufof(f"ps{pj2}")],
                  note=tb + f"w2mm{hoc}")
            S.add("vector", lambda e, pj2=pj2, hoc=hoc: e.scalar_tensor_tensor(
                      hA[:, hoc * T:(hoc + 1) * T], ps_mm[pj2][:, :],
                      csb["w2s"][:, hoc:hoc + 1], a_scr[hoc % 2][:, :], ALU.mult, ALU.mult),
                  reads=[bufof(f"ps{pj2}"), bufof(f"a_scr{hoc % 2}"), bufof("consts")],
                  writes=[R1("hA", hoc)], note=tb + f"w2ev{hoc}")

        def add_h(hoc, key, srcb, dstb, scol, note):
            j = hoc % 2
            pj = hoc % 4
            wslab(key, hoc, wffn, j, tb + f"{key}{hoc}")
            def hmm(e, j=j, pj=pj, srcb=srcb):
                last = None
                for k in range(NOC):
                    last = e.matmul(ps_mm[pj][:, :], wffn[j][:, k * 128:(k + 1) * 128],
                                    srcb[:, k * T:(k + 1) * T], start=(k == 0), stop=(k == NOC - 1))
                return last
            S.add("tensor", hmm, reads=[bufof(f"wffn{j}")] + (RHA if srcb is hA else RHB),
                  writes=[bufof(f"ps{pj}")], note=tb + f"{note}mm{hoc}")
            S.add("scalar", lambda e, pj=pj, hoc=hoc, scol=scol, dstb=dstb: e.activation(
                      out=dstb[:, hoc * T:(hoc + 1) * T], in_=ps_mm[pj][:, :], func=AF.Gelu,
                      scale=csb[scol][:, hoc:hoc + 1], bias=csb["zerof"][:, 0:1]),
                  reads=[bufof(f"ps{pj}"), bufof("consts")],
                  writes=[R1("hA" if dstb is hA else "hB", hoc)], note=tb + f"{note}ev{hoc}")

        def add_w3(oc3):
            j = oc3 % 2
            pj = oc3 % 4
            wslab("w3f", oc3, wffn, j, tb + f"w3f{oc3}")
            def w3mm(e, j=j, pj=pj):
                last = None
                for k in range(NOC):
                    last = e.matmul(ps_mm[pj][:, :], wffn[j][:, k * 128:(k + 1) * 128],
                                    hA[:, k * T:(k + 1) * T], start=(k == 0), stop=(k == NOC - 1))
                return last
            S.add("tensor", w3mm, reads=[bufof(f"wffn{j}")] + RHA,
                  writes=[bufof(f"ps{pj}")], note=tb + f"w3mm{oc3}")
            S.add("vector", lambda e, pj=pj, oc3=oc3: e.scalar_tensor_tensor(
                      fm(attn, oc3), ps_mm[pj][:, :], csb["w3s"][:, oc3:oc3 + 1],
                      fm(attn, oc3), ALU.mult, ALU.add),
                  reads=[bufof(f"ps{pj}"), bufof("consts"), R1("at", oc3)], writes=[R1("at", oc3)],
                  note=tb + f"w3ev{oc3}")

        for hoc in range(NOC):
            ffn_steps.append(lambda hoc=hoc: add_w12(hoc))
        for hoc in range(NOC):
            ffn_steps.append(lambda hoc=hoc: add_h(hoc, "h1f", hA, hB, "h1s", "h1"))
        for hoc in range(NOC):
            ffn_steps.append(lambda hoc=hoc: add_h(hoc, "h2f", hB, hA, "h2s", "h2"))

        def add_scan(n):
            S.add("tensor", lambda e, n=n: e.matmul(
                      ps_bcB[:, :], csb["idb"][0:16, n:n + 1].broadcast_to([16, 128]),
                      Bt_sb[:, :], start=True, stop=True),
                  reads=[bufof("Bt"), bufof("consts")], writes=[bufof("psB")], note=tb + f"bcB{n}")
            S.add("tensor", lambda e, n=n: e.matmul(
                      ps_bcC[:, :], csb["idb"][0:16, n:n + 1].broadcast_to([16, 128]),
                      Ct_sb[:, :], start=True, stop=True),
                  reads=[bufof("Ct"), bufof("consts")], writes=[bufof("psC")], note=tb + f"bcC{n}")
            for p in range(NDC // 2):
                dcs = (2 * p, 2 * p + 1)
                for dc in dcs:
                    ja = (n * NDC + dc) % 4
                    S.add("scalar", lambda e, n=n, dc=dc, ja=ja: e.activation(
                              out=aS[ja][:, :], in_=fm(dt_bf, dc), func=AF.Exp,
                              scale=A_sb[:, dc * NST + n:dc * NST + n + 1], bias=csb["zerof"][:, 0:1]),
                          reads=[R1("dt", dc), bufof("A")], writes=[bufof(f"aS{ja}")],
                          note=tb + f"dA{n}.{dc}")
                for dc in dcs:
                    S.add("vector", lambda e, dc=dc: e.tensor_mul(
                              bS[dc % 2][:, :], fm(w_all, dc), ps_bcB[:, :]),
                          reads=[R1("wal", dc), bufof("psB")], writes=[bufof(f"bS{dc % 2}")],
                          note=tb + f"bmul{n}.{dc}")
                for dc in dcs:
                    ja = (n * NDC + dc) % 4
                    S.add("vector", lambda e, n=n, dc=dc, ja=ja: e.tensor_tensor_scan(
                              hS[dc % 2][:, :], aS[ja][:, :], bS[dc % 2][:, :],
                              hstate[:, dc * NST + n:dc * NST + n + 1], ALU.mult, ALU.add),
                          reads=[bufof(f"aS{ja}"), bufof(f"bS{dc % 2}"), R1("hst", dc)],
                          writes=[bufof(f"hS{dc % 2}")], note=tb + f"scan{n}.{dc}")
                for dc in dcs:
                    S.add("vector", lambda e, dc=dc: e.tensor_mul(
                              bS[dc % 2][:, :], hS[dc % 2][:, :], ps_bcC[:, :]),
                          reads=[bufof(f"hS{dc % 2}"), bufof("psC")], writes=[bufof(f"bS{dc % 2}")],
                          note=tb + f"ymul{n}.{dc}")
                for dc in dcs:
                    S.add("vector", lambda e, n=n, dc=dc: e.tensor_copy(
                              hstate[:, dc * NST + n:dc * NST + n + 1], hS[dc % 2][:, T - 1:T]),
                          reads=[bufof(f"hS{dc % 2}")], writes=[R1("hst", dc)],
                          note=tb + f"hsv{n}.{dc}")
                for dc in dcs:
                    S.add("gpsimd", lambda e, dc=dc: e.tensor_add(
                              fm(y_f, dc), fm(y_f, dc), bS[dc % 2][:, :]),
                          reads=[R1("y", dc), bufof(f"bS{dc % 2}")], writes=[R1("y", dc)],
                          note=tb + f"yacc{n}.{dc}", dma=False)

        fi = 0
        front_done = False
        for n in range(NST):
            add_scan(n)
            take = 6 if n < NST - 1 else len(ffn_steps) - fi
            for _ in range(take):
                if fi < len(ffn_steps):
                    ffn_steps[fi]()
                    fi += 1
                    if fi == 40 and not front_done and i + 1 < n_tiles:
                        emit_front(i + 1)
                        front_done = True
        while fi < len(ffn_steps):
            ffn_steps[fi]()
            fi += 1
        if not front_done and i + 1 < n_tiles:
            emit_front(i + 1)

        if stage < 6:
            continue
        for dc in range(NDC):
            S.add("vector", lambda e, dc=dc: e.scalar_tensor_tensor(
                      fm(y_f, dc), fm(u_bf, dc), csb["dsk"][:, dc:dc + 1],
                      fm(y_f, dc), ALU.mult, ALU.add),
                  reads=[R1("u", dc), bufof("consts"), R1("y", dc)],
                  writes=[R1("y", dc)], note=tb + f"gate{dc}a")
        for dc in range(NDC):
            S.add("vector", lambda e, dc=dc: e.tensor_mul(
                      fm(u_bf, dc), fm(y_f, dc),
                      x1[:, dc * (T + 2) + 1: dc * (T + 2) + 1 + T]),
                  reads=[R1("y", dc), R1("x1r", dc)],
                  writes=[R1("u", dc)], note=tb + f"gate{dc}b")
        for oc in range(NDC):
            j = (oc // 4) % 2
            if oc % 4 == 0:
                wslab4("w2m", oc // 4, j, tb + f"w2m{oc}")
            pj = oc % 4
            S.add("tensor", lambda e, j=j, oc=oc, pj=pj: mm_group(e, j, (oc % 4) * D, pj, u_bf),
                  reads=[bufof(f"wffn{j}")] + RU, writes=[bufof(f"ps{pj}")],
                  note=tb + f"w2amm{oc}")
            S.add("scalar", lambda e, pj=pj, oc=oc: e.add(
                      fm(attn, oc), ps_mm[pj][:, :], csb["w2b"][:, oc:oc + 1]),
                  reads=[bufof(f"ps{pj}"), bufof("consts")], writes=[R1("at", oc)],
                  note=tb + f"w2aev{oc}")

        if stage < 7:
            continue
        for oc3 in range(NDC):
            add_w3(oc3)

        if stage < 8:
            continue
        for tc in range(NTC):
            S.add("gpsimd",
                  lambda e, i=i, tc=tc: e.dma_start(
                      out=xtl[tc % 2][:, :],
                      in_=x_d[i * T + tc * 128: i * T + (tc + 1) * 128, :]),
                  reads=[], writes=[bufof(f"xtl{tc % 2}")], note=tb + f"xtload{tc}")
            def trS(e, tc=tc):
                last = None
                for dc in range(NDC):
                    last = e.transpose(ps_tail[:, dc * 128:(dc + 1) * 128],
                                       attn[:, dc * T + tc * 128: dc * T + (tc + 1) * 128],
                                       csb["idb"][:, :])
                return last
            S.add("tensor", trS, reads=RAT + [bufof("consts")], writes=[bufof("psT")],
                  note=tb + f"trS{tc}")
            jo = tc % 2
            S.add("vector", lambda e, tc=tc, jo=jo: e.tensor_add(
                      out_sb[jo][:, :], ps_tail[:, :], xtl[tc % 2][:, :]),
                  reads=[bufof("psT"), bufof(f"xtl{tc % 2}")], writes=[bufof(f"out_sb{jo}")],
                  note=tb + f"oadd{tc}")
            S.add("gpsimd", lambda e, i=i, tc=tc, jo=jo: e.dma_start(
                      out=out_d[i * T + tc * 128: i * T + (tc + 1) * 128, :], in_=out_sb[jo][:, :]),
                  reads=[bufof(f"out_sb{jo}")], writes=[], note=tb + f"ostore{tc}")

    dbg_map = dict(xn_fm=xn_fm, xm_fm=xm_fm, x1=x1, u_bf=u_bf, dt_bf=dt_bf,
                   y_f=y_f, attn=attn, hA=hA, hB=hB, w_all=w_all,
                   xn_tm=xn_tm, ex_st=ex_st, out_sb1=out_sb[1], x_tm=xfr[0])
    for name in debug_taps:
        buf = dbg_map[name]
        shp = list(buf.shape)
        dt = buf.dtype
        od = nc.dram_tensor("dbg_" + name, shp, dt, kind="ExternalOutput").ap()
        S.add("gpsimd", lambda e, od=od, buf=buf: e.dma_start(out=od[:, :], in_=buf[:, :]),
              reads=[bufof({"xn_fm": "xn_fm", "xm_fm": "xmf0", "x1": "x1r0", "u_bf": "u0",
                            "dt_bf": "dt0", "y_f": "y0", "attn": "at0", "hA": "hA0",
                            "hB": "hB0", "w_all": "wal0", "x_tm": "xfr0", "xn_tm": "xn_tm",
                            "ex_st": "ex_st", "out_sb1": "out_sb1"}[name])],
              note=f"dbg {name}")

    S.finalize()
    sems = {}
    semctx = []
    semnames = list(ENGINES) + [e + str(j) for e, k in _DMA_RING.items() for j in range(k)]
    for e in semnames:
        h = nc.semaphore("sem_" + e)
        semctx.append(h)
        sems[e] = h.__enter__()
    blk = nc.Block()
    block = blk.__enter__()
    for ename in ENGINES:
        def body(eng, ename=ename):
            S.emit_engine(ename, eng, sems)
            if ename in _DMA_RING:
                for j in range(_DMA_RING[ename]):
                    nm = ename + str(j)
                    if S.totals.get(nm):
                        eng.wait_ge(sems[nm], S.totals[nm])
        getattr(block, ename)(body)
    blk.__exit__(None, None, None)
    for h in reversed(semctx):
        h.__exit__(None, None, None)
    for h in reversed(ctx):
        h.__exit__(None, None, None)
    return nc


_CACHE = {}


def _get_program():
    if "nc" not in _CACHE:
        _CACHE["nc"] = build_program()
    return _CACHE["nc"]


def kernel(**inputs):
    nc = _get_program()
    hp = host_prep(inputs)
    x = np.asarray(inputs["L0_attn_x"], np.float32)
    in_maps = []
    for b in range(8):
        m = dict(hp)
        m["x"] = np.ascontiguousarray(x[b])
        in_maps.append(m)
    res = run_bass_kernel_spmd(nc, in_maps, core_ids=list(range(8)))
    return np.stack([r["out"] for r in res.results], axis=0)


# version 08
# speedup vs baseline: 1.0321x; 1.0321x over previous
import numpy as np
import ml_dtypes
import concourse.bass as bass
import concourse.mybir as mybir
from concourse.bass_utils import run_bass_kernel_spmd

F32 = mybir.dt.float32
BF16 = mybir.dt.bfloat16
AF = mybir.ActivationFunctionType
ALU = mybir.AluOpType
AX = mybir.AxisListType

L, D, NST, HID = 8192, 1024, 16, 4096
T = 512
NTC = T // 128
NDC = D // 128
NOC = HID // 128
EPS = 1e-5
RD = 1.0 / D

ENGINES = ("sync", "gpsimd", "tensor", "vector", "scalar")
_UNIT = {"sync": 16, "gpsimd": 16, "tensor": 1, "vector": 1, "scalar": 1}
_DMA_RING = {"sync": 6, "gpsimd": 3}


class Buf:
    __slots__ = ("name", "last_w", "readers")

    def __init__(self, name):
        self.name = name
        self.last_w = None
        self.readers = []


class Sched:
    def __init__(self):
        self.steps = []

    def add(self, eng, emit, reads=(), writes=(), note="", dma=None):
        if dma is None:
            dma = eng in ("sync", "gpsimd")
        sid = len(self.steps)
        deps = set()
        for b in reads:
            if b.last_w is not None:
                deps.add(b.last_w)
        for b in writes:
            if b.last_w is not None:
                deps.add(b.last_w)
            for r in b.readers:
                deps.add(r)
        self.steps.append(dict(id=sid, eng=eng, emit=emit, deps=deps, dma=dma,
                               note=note, needed=False, cum=0))
        for b in reads:
            b.readers.append(sid)
        for b in writes:
            b.last_w = sid
            b.readers = []
        return sid

    def finalize(self):
        steps = self.steps
        eng_of = [s["eng"] for s in steps]
        for s in steps:
            for d in s["deps"]:
                if eng_of[d] != s["eng"] or steps[d]["dma"]:
                    steps[d]["needed"] = True
        for s in steps:
            if s["dma"]:
                s["needed"] = True
        cum = {}
        eseq = {e: 0 for e in ENGINES}
        dma_ct = {e: 0 for e in _DMA_RING}
        for s in steps:
            if s["dma"]:
                j = dma_ct[s["eng"]]
                dma_ct[s["eng"]] += 1
                s["sem"] = s["eng"] + str(j % _DMA_RING[s["eng"]])
            else:
                s["sem"] = s["eng"]
            if s["needed"]:
                cum[s["sem"]] = cum.get(s["sem"], 0) + _UNIT[s["eng"]]
            s["cum"] = cum.get(s["sem"], 0)
            s["eseq"] = eseq[s["eng"]]
            eseq[s["eng"]] += 1
        self.totals = dict(cum)

    def emit_engine(self, ename, eng, sems):
        steps = self.steps
        guar = {}
        drains_ok = ename in ("vector", "scalar", "gpsimd")
        drained_upto = 0
        n = 0
        for s in steps:
            if s["eng"] != ename:
                continue
            need = {}
            same_need_drain = False
            for d in s["deps"]:
                de = steps[d]["eng"]
                if de == ename and not steps[d]["dma"]:
                    if (drains_ok and not s["dma"] and steps[d]["eseq"] >= drained_upto
                            and steps[d]["eseq"] >= s["eseq"] - 1):
                        same_need_drain = True
                    continue
                assert steps[d]["needed"], (s["note"], steps[d]["note"])
                sm = steps[d]["sem"]
                t = steps[d]["cum"]
                if t > guar.get(sm, 0) and t > need.get(sm, 0):
                    need[sm] = t
            if same_need_drain:
                eng.drain()
                drained_upto = s["eseq"]
            for sm in sorted(need):
                eng.wait_ge(sems[sm], need[sm])
                guar[sm] = need[sm]
            last = s["emit"](eng)
            if s["needed"]:
                assert last is not None, f"step {s['id']} {s['note']} returned no instruction"
                last.then_inc(sems[s["sem"]], _UNIT[ename])
            n += 1
        return n


def host_prep(inputs):
    g = lambda k: np.asarray(inputs[k], dtype=np.float32)

    def prep_mix(w):
        return np.ascontiguousarray(
            w.reshape(NDC, 128, NDC, 128).transpose(2, 1, 0, 3).reshape(NDC, 128, D))

    def prep_T(w):
        O, I = w.shape
        no, ni = O // 128, I // 128
        return np.ascontiguousarray(
            w.reshape(no, 128, ni, 128).transpose(0, 3, 2, 1).reshape(no, 128, I))

    def cols(v):
        return np.ascontiguousarray(np.asarray(v, np.float32).reshape(-1, 128).T)

    d = {}
    d["w1m"] = prep_mix(g("L0_attn_w1_w"))
    d["v1m"] = prep_mix(g("L0_attn_v1_w"))
    d["wdtm"] = prep_mix(g("L0_attn_W_dt"))
    d["w2m"] = prep_mix(g("L0_attn_w2_w"))
    d["w1f"] = prep_T(g("L0_ffn_w1_w"))
    d["w2f"] = prep_T(g("L0_ffn_w2_w"))
    d["h1f"] = prep_T(g("L0_ffn_h1_w"))
    d["h2f"] = prep_T(g("L0_ffn_h2_w"))
    d["w3f"] = prep_T(g("L0_ffn_w3_w"))
    d["alog"] = np.ascontiguousarray(
        g("L0_attn_A_log").reshape(NDC, 128, NST).transpose(1, 0, 2).reshape(128, NDC * NST))
    d["convw"] = np.ascontiguousarray(
        g("L0_attn_conv_w").reshape(D, 3).reshape(NDC, 128, 3).transpose(1, 0, 2).reshape(128, NDC * 3))
    d["dsk"] = cols(g("L0_attn_D_skip"))
    d["w1b"] = cols(g("L0_attn_w1_b"))
    d["v1b"] = cols(g("L0_attn_v1_b"))
    d["w2b"] = cols(g("L0_attn_w2_b"))
    d["bdt"] = cols(g("L0_attn_b_dt"))
    d["w1s"] = cols(g("L0_ffn_w1_s"))
    d["w2s"] = cols(g("L0_ffn_w2_s"))
    d["h1s"] = cols(g("L0_ffn_h1_s"))
    d["h2s"] = cols(g("L0_ffn_h2_s"))
    d["w3s"] = cols(g("L0_ffn_w3_s"))
    d["wb"] = np.ascontiguousarray(
        g("L0_attn_W_B").reshape(NDC, 128, NST).transpose(1, 0, 2).reshape(128, NDC * NST))
    d["wc"] = np.ascontiguousarray(
        g("L0_attn_W_C").reshape(NDC, 128, NST).transpose(1, 0, 2).reshape(128, NDC * NST))
    d["gcol"] = cols(g("L0_attn_ln_g"))
    d["bcol"] = cols(g("L0_attn_ln_b"))
    d["onesf"] = np.ones((128, 1), np.float32)
    d["zerof"] = np.zeros((128, 1), np.float32)
    d["epsf"] = np.full((128, 1), 1e-5, np.float32)
    d["onecol"] = np.ones((1, 1), ml_dtypes.bfloat16)
    d["idb"] = np.eye(128, dtype=ml_dtypes.bfloat16)
    return d


def build_program(n_tiles=L // T, debug_taps=(), stage=99):
    nc = bass.Bass("TRN2", target_bir_lowering=False)

    x_d = nc.dram_tensor("x", [L, D], F32, kind="ExternalInput").ap()
    out_d = nc.dram_tensor("out", [L, D], F32, kind="ExternalOutput").ap()

    wspec = dict(w1m=(NDC, 128, D), v1m=(NDC, 128, D), wdtm=(NDC, 128, D),
                 w2m=(NDC, 128, D), w1f=(NOC, 128, D), w2f=(NOC, 128, D),
                 h1f=(NOC, 128, HID), h2f=(NOC, 128, HID), w3f=(NDC, 128, HID))
    wd = {}
    ws = {}
    for k, shp in wspec.items():
        wd[k] = nc.dram_tensor(k, list(shp), F32, kind="ExternalInput").ap()
        ws[k] = nc.dram_tensor(k + "S", list(shp), BF16, kind="Internal").ap()

    cspec_f32 = dict(alog=(128, NDC * NST), convw=(128, NDC * 3), dsk=(128, NDC),
                     w1b=(128, NDC), v1b=(128, NDC), w2b=(128, NDC), bdt=(128, NDC),
                     w1s=(128, NOC), w2s=(128, NOC), h1s=(128, NOC), h2s=(128, NOC),
                     w3s=(128, NDC), gcol=(128, NDC), bcol=(128, NDC))
    cd = {k: nc.dram_tensor(k, list(s), F32, kind="ExternalInput").ap()
          for k, s in cspec_f32.items()}
    cd["wb"] = nc.dram_tensor("wb", [128, NDC * NST], F32, kind="ExternalInput").ap()
    cd["wc"] = nc.dram_tensor("wc", [128, NDC * NST], F32, kind="ExternalInput").ap()

    cd["onecol"] = nc.dram_tensor("onecol", [1, 1], BF16, kind="ExternalInput").ap()
    cd["idb"] = nc.dram_tensor("idb", [128, 128], BF16, kind="ExternalInput").ap()
    cd["onesf"] = nc.dram_tensor("onesf", [128, 1], F32, kind="ExternalInput").ap()
    cd["zerof"] = nc.dram_tensor("zerof", [128, 1], F32, kind="ExternalInput").ap()
    cd["epsf"] = nc.dram_tensor("epsf", [128, 1], F32, kind="ExternalInput").ap()

    ctx = []

    def sb(name, shape, dt):
        h = nc.sbuf_tensor(name, list(shape), dt)
        ctx.append(h)
        return h.__enter__()

    def psum(name, shape, dt):
        h = nc.psum_tensor(name, list(shape), dt)
        ctx.append(h)
        return h.__enter__()

    xfr = [sb(f"xfr{j}", [128, D], F32) for j in range(2)]
    xtl = [sb(f"xtl{j}", [128, D], F32) for j in range(2)]
    xn_tm = sb("xn_tm", [128, D], BF16)
    xm_tm = xn_tm
    stats = sb("stats", [128, 32], F32)
    xn_fm = sb("xn_fm", [128, NDC * T], BF16)
    xm_fm = sb("xm_fm", [128, NDC * T], BF16)
    x1 = sb("x1", [128, NDC * (T + 2)], BF16)
    u_bf = sb("u_bf", [128, NDC * T], BF16)
    dt_bf = sb("dt_bf", [128, NDC * T], BF16)
    w_all = sb("w_all", [128, NDC * T], BF16)
    y_f = sb("y_f", [128, NDC * T], F32)
    attn = sb("attn", [128, NDC * T], BF16)
    hA = sb("hA", [128, NOC * T], BF16)
    hB = sb("hB", [128, NOC * T], BF16)
    aS = [sb(f"aS{j}", [128, T], F32) for j in range(4)]
    bS = [sb(f"bS{j}", [128, T], F32) for j in range(2)]
    hS = [sb(f"hS{j}", [128, T], F32) for j in range(2)]
    hstate = sb("hstate", [128, NDC * NST], F32)
    x1b = sb("x1b", [128, NDC], BF16)
    xcol = sb("xcol", [128, NDC], BF16)
    Bt_sb = sb("Bt_sb", [16, T], BF16)
    Ct_sb = sb("Ct_sb", [16, T], BF16)
    a_scr = [sb(f"a_scr{j}", [128, T], BF16) for j in range(2)]
    wffn = [sb(f"wffn{j}", [128, HID], BF16) for j in range(2)]
    wmix = wffn
    out_sb = [sb(f"out_sb{j}", [128, D], F32) for j in range(2)]
    s_scr = [sb("s_scr0", [128, T], F32)]
    x_ex = out_sb[1][0:1, :]
    xn_ex = x_ex
    xm_ex = xm_tm
    ex_st = sb("ex_st", [1, 32], F32)
    WBb = sb("WBb", [128, NDC * NST], BF16)
    WCb = sb("WCb", [128, NDC * NST], BF16)
    csb = {}
    for k, s in cspec_f32.items():
        csb[k] = sb("c_" + k, list(s), F32)
    csb["onecol"] = sb("c_onecol", [1, 1], BF16)
    csb["idb"] = sb("c_idb", [128, 128], BF16)
    csb["onesf"] = sb("c_onesf", [128, 1], F32)
    csb["zerof"] = sb("c_zerof", [128, 1], F32)
    csb["epsf"] = sb("c_epsf", [128, 1], F32)

    ps_mm = [psum(f"ps_mm{j}", [128, T], F32) for j in range(3)]
    ps_bcBd = [psum(f"ps_bcB{j}", [128, T], F32) for j in range(2)]
    ps_bcCd = [psum(f"ps_bcC{j}", [128, T], F32) for j in range(2)]
    ps_bcB = ps_bcBd[0]
    ps_bcC = ps_bcCd[0]
    ps_tail = psum("ps_tail", [128, D], BF16)

    B = {}

    def bufof(name):
        if name not in B:
            B[name] = Buf(name)
        return B[name]

    bufof("xcol")
    for nm in ("xn_tm xm_tm stats xn_fm x1b Bt Ct x_ex xm_ex ex_st "
               "WBb WCb consts psB psC psT xfr0 xfr1 xtl0 xtl1").split():
        bufof(nm)

    def R(base, n):
        return [bufof(f"{base}{i}") for i in range(n)]

    def R1(base, i):
        return bufof(f"{base}{i}")
    RY, RDT, RW, RU, RX1, RXM, RAT, RHS = (R("y", NDC), R("dt", NDC), R("wal", NDC),
                                           R("u", NDC), R("x1r", NDC), R("xmf", NDC),
                                           R("at", NDC), R("hst", NDC))
    RHA, RHB = R("hA", NOC), R("hB", NOC)
    for j in range(2):
        for nm in ("a_scr", "out_sb", "wffn", "bS", "hS"):
            bufof(f"{nm}{j}")
    for j in range(4):
        bufof(f"aS{j}")
    bufof("s_scr0")
    for j in range(3):
        bufof(f"ps{j}")
    for j in range(2):
        bufof(f"psB{j}")
        bufof(f"psC{j}")
    for k in wspec:
        bufof("scr_" + k)

    S = Sched()

    def c_load(k):
        S.add("sync", lambda e, k=k: e.dma_start(out=csb[k][:, :], in_=cd[k][:, :]),
              writes=[bufof("consts")], note=f"load {k}")
    for k in list(cspec_f32) + ["onecol", "idb", "onesf", "zerof", "epsf"]:
        c_load(k)
    for k in wspec:
        S.add("gpsimd", lambda e, k=k: e.dma_start(out=ws[k][:, :, :], in_=wd[k][:, :, :]),
              writes=[bufof("scr_" + k)], note=f"cast {k}")
    A_sb = csb["alog"]
    bufof("A")
    S.add("scalar", lambda e: e.activation(out=A_sb[:, :], in_=csb["alog"][:, :], func=AF.Exp),
          reads=[bufof("consts")], writes=[bufof("A")], note="expA")
    S.add("vector", lambda e: e.tensor_scalar_mul(A_sb[:, :], A_sb[:, :], -1.0),
          reads=[bufof("A")], writes=[bufof("A")], note="negA")
    S.add("gpsimd", lambda e: e.dma_start(out=WBb[:, :], in_=cd["wb"][:, :]),
          writes=[bufof("WBb")], note="WBb")
    S.add("gpsimd", lambda e: e.dma_start(out=WCb[:, :], in_=cd["wc"][:, :]),
          writes=[bufof("WCb")], note="WCb")
    S.add("vector", lambda e: e.memset(hstate[:, :], 0.0), writes=[bufof("hstate")], note="hs0")
    S.add("vector", lambda e: e.memset(x1b[:, :], 0.0), writes=[bufof("x1b")], note="x1b0")

    def fm(bufh, dc, lo=0, hi=T):
        return bufh[:, dc * T + lo: dc * T + hi]

    ACC = dict(mult=ALU.mult, add=ALU.add, subtract=ALU.subtract)

    def emit_front(i):
        tb = f"t{i}."
        if i + 1 < L // T:
            S.add("gpsimd",
                  lambda e, i=i: e.dma_start(out=x_ex, in_=x_d[(i + 1) * T:(i + 1) * T + 1, :]),
                  writes=[bufof("out_sb1")], note=tb + "xex")

        for tc in range(NTC):
            S.add("gpsimd",
                  lambda e, i=i, tc=tc: e.dma_start(
                      out=xfr[tc % 2][:, :],
                      in_=x_d[i * T + tc * 128: i * T + (tc + 1) * 128, :]),
                  reads=[], writes=[bufof(f"xfr{tc % 2}")], note=tb + f"xload{tc}")
            xch = xfr[tc % 2][:, :]
            xbuf = bufof(f"xfr{tc % 2}")
            st = stats

            def bn_steps(srcap, srcbuf, mcol, nt, npart=128):
                S.add("vector", lambda e: e.bn_stats(st[0:npart, 16:22], srcap[:, 0:512]),
                      reads=[srcbuf], writes=[bufof("stats")], note=nt + "bnA")
                S.add("vector", lambda e: e.bn_stats(st[0:npart, 22:28], srcap[:, 512:1024]),
                      reads=[srcbuf], writes=[bufof("stats")], note=nt + "bnB")
                S.add("vector", lambda e: e.bn_aggr(st[0:npart, mcol:mcol + 2], st[0:npart, 16:28]),
                      reads=[bufof("stats")], writes=[bufof("stats")], note=nt + "bnG")

            def rstd_steps(s, v, t, r, sbuf, nt):
                npart = s.shape[0]
                S.add("scalar", lambda e: e.activation(out=s[:, t:t + 1], in_=s[:, v:v + 1], func=AF.Ln,
                                                       bias=csb["epsf"][0:npart, 0:1],
                                                       scale=csb["onesf"][0:npart, 0:1]),
                      reads=[sbuf], writes=[sbuf], note=nt + "ln")
                S.add("scalar", lambda e: e.activation(out=s[:, r:r + 1], in_=s[:, t:t + 1],
                                                       func=AF.Exp, scale=-0.5),
                      reads=[sbuf], writes=[sbuf], note=nt + "exp")
            bn_steps(xch, xbuf, 2, tb + "v1.")
            rstd_steps(st, 3, 4, 5, bufof("stats"), tb + "r1.")
            S.add("vector", lambda e, xch=xch, s=st: e.tensor_scalar(
                      xn_tm[:, :], xch, s[:, 2:3], s[:, 5:6], ALU.subtract, ALU.mult),
                  reads=[xbuf, bufof("stats")], writes=[bufof("xn_tm")], note=tb + "xn")
            bn_steps(xn_tm[:, :], bufof("xn_tm"), 10, tb + "v2.")
            rstd_steps(st, 11, 12, 13, bufof("stats"), tb + "r2.")
            def tr(e, src):
                last = None
                for dc in range(NDC):
                    last = e.transpose(ps_tail[:, dc * 128:(dc + 1) * 128],
                                       src[:, dc * 128:(dc + 1) * 128], csb["idb"][:, :])
                return last
            def ev(e, dst, tcv):
                src3 = ps_tail[:, :].rearrange("p (dc t) -> p dc t", dc=NDC)
                dst3 = dst[:, :].rearrange("p (dc t) -> p dc t", dc=NDC)[:, :, tcv * 128:(tcv + 1) * 128]
                return e.copy(dst3, src3)
            S.add("tensor", lambda e, tr=tr: tr(e, xn_tm),
                  reads=[bufof("xn_tm"), bufof("consts")], writes=[bufof("psT")], note=tb + "trxn")
            S.add("scalar", lambda e, ev=ev, tc=tc: ev(e, xn_fm, tc),
                  reads=[bufof("psT")], writes=[bufof("xn_fm")], note=tb + "evxn")
            S.add("vector", lambda e, s=st: e.tensor_scalar(
                      xm_tm[:, :], xn_tm[:, :], s[:, 10:11], s[:, 13:14],
                      ALU.subtract, ALU.mult),
                  reads=[bufof("xn_tm"), bufof("stats")],
                  writes=[bufof("xn_tm")], note=tb + "xm1")
            S.add("tensor", lambda e, tr=tr: tr(e, xm_tm),
                  reads=[bufof("xn_tm"), bufof("consts")], writes=[bufof("psT")], note=tb + "trxm")
            def evxm(e, tcv):
                src3 = ps_tail[:, :].rearrange("p (dc t) -> p dc t", dc=NDC)
                last = None
                for dc in range(NDC):
                    last = e.tensor_scalar(
                        xm_fm[:, dc * T + tcv * 128: dc * T + (tcv + 1) * 128],
                        src3[:, dc, :], csb["gcol"][:, dc:dc + 1], csb["bcol"][:, dc:dc + 1],
                        ALU.mult, ALU.add)
                return last
            S.add("vector", lambda e, evxm=evxm, tc=tc: evxm(e, tc),
                  reads=[bufof("psT"), bufof("consts")], writes=RXM, note=tb + "evxm")

        if i + 1 < L // T:
            es = ex_st
            def exbn_steps(srcap, srcbuf, mcol, nt):
                S.add("vector", lambda e: e.bn_stats(es[:, 16:22], srcap[:, 0:512]),
                      reads=[srcbuf], writes=[bufof("ex_st")], note=nt + "bnA")
                S.add("vector", lambda e: e.bn_stats(es[:, 22:28], srcap[:, 512:1024]),
                      reads=[srcbuf], writes=[bufof("ex_st")], note=nt + "bnB")
                S.add("vector", lambda e: e.bn_aggr(es[:, mcol:mcol + 2], es[:, 16:28]),
                      reads=[bufof("ex_st")], writes=[bufof("ex_st")], note=nt + "bnG")
            exbn_steps(x_ex, bufof("out_sb1"), 2, tb + "exv1.")
            rstd_steps(es, 3, 4, 5, bufof("ex_st"), tb + "exr1.")
            S.add("vector", lambda e: e.tensor_scalar(
                      xn_ex, x_ex, es[:, 2:3], es[:, 5:6], ALU.subtract, ALU.mult),
                  reads=[bufof("out_sb1"), bufof("ex_st")], writes=[bufof("out_sb1")], note=tb + "exxn")
            exbn_steps(xn_ex, bufof("out_sb1"), 10, tb + "exv2.")
            rstd_steps(es, 11, 12, 13, bufof("ex_st"), tb + "exr2.")
            S.add("vector", lambda e: e.tensor_scalar(
                      xn_ex, xn_ex, es[:, 10:11], es[:, 13:14], ALU.subtract, ALU.mult),
                  reads=[bufof("out_sb1"), bufof("ex_st")],
                  writes=[bufof("out_sb1")], note=tb + "exxm1")
            S.add("vector", lambda e: e.tensor_copy(xm_ex[0:1, :], xn_ex),
                  reads=[bufof("out_sb1")], writes=[bufof("xn_tm")], note=tb + "exxm2")
            def exmm(e):
                last = None
                for dc in range(NDC):
                    last = e.matmul(ps_bcC[:, dc:dc + 1], xm_ex[0:1, dc * 128:(dc + 1) * 128],
                                    csb["onecol"][0:1, 0:1], start=True, stop=True)
                return last
            S.add("tensor", exmm, reads=[bufof("xn_tm"), bufof("consts")],
                  writes=[bufof("psC0")], note=tb + "exmm")
            def excol(e):
                last = None
                for dc in range(NDC):
                    last = e.tensor_scalar(xcol[:, dc:dc + 1], ps_bcC[:, dc:dc + 1],
                                           csb["gcol"][:, dc:dc + 1], csb["bcol"][:, dc:dc + 1],
                                           ALU.mult, ALU.add)
                return last
            S.add("vector", excol,
                  reads=[bufof("psC0"), bufof("consts")], writes=[bufof("xcol")], note=tb + "excol")


    for i in range(n_tiles):
        tb = f"t{i}."
        if i == 0:
            emit_front(0)
        def wslab(key, idx, bufset, j, nslab_note):
            S.add("sync",
                  lambda e, key=key, idx=idx, j=j: e.dma_start(out=wffn[j][:, 0:ws[key].shape[2]],
                                                               in_=ws[key][idx, :, :]),
                  reads=[bufof("scr_" + key)], writes=[bufof(f"wffn{j}")],
                  note=nslab_note)

        def wslab4(key, g, j, nslab_note):
            S.add("sync",
                  lambda e, key=key, g=g, j=j: e.dma_start(
                      out=wffn[j][:, :].rearrange("p (s f) -> p s f", s=4),
                      in_=ws[key][4 * g:4 * g + 4, :, :].rearrange("s p f -> p s f")),
                  reads=[bufof("scr_" + key)], writes=[bufof(f"wffn{j}")],
                  note=nslab_note)

        def mm_group(e, j, soff, pj, src, nk=NDC):
            last = None
            for k in range(nk):
                last = e.matmul(ps_mm[pj][:, :], wffn[j][:, soff + k * 128: soff + (k + 1) * 128],
                                fm(src, k), start=(k == 0), stop=(k == nk - 1))
            return last

        has_ex = stage >= 1 and i + 1 < L // T
        for g in range(2 if stage >= 1 else 0):
            j = g % 2
            wslab4("w1m", g, j, tb + f"w1m{g}")
            for s4 in range(4):
                oc = 4 * g + s4
                pj = oc % 3
                S.add("tensor", lambda e, j=j, s4=s4, pj=pj: mm_group(e, j, s4 * D, pj, xm_fm),
                      reads=[bufof(f"wffn{j}")] + RXM,
                      writes=[bufof(f"ps{pj}")], note=tb + f"x1mm{oc}")
                if has_ex:
                    def exg(e, j=j, s4=s4, oc=oc):
                        last = None
                        for k in range(NDC):
                            last = e.matmul(ps_bcB[:, oc:oc + 1],
                                            wffn[j][:, s4 * D + k * 128: s4 * D + (k + 1) * 128],
                                            xcol[:, k:k + 1], start=(k == 0), stop=(k == NDC - 1))
                        return last
                    S.add("tensor", exg, reads=[bufof(f"wffn{j}"), bufof("xcol")],
                          writes=[bufof("psB0")], note=tb + f"x1exmm{oc}")
                S.add("scalar", lambda e, pj=pj, oc=oc: e.add(
                          x1[:, oc * (T + 2) + 1: oc * (T + 2) + 1 + T], ps_mm[pj][:, :],
                          csb["w1b"][:, oc:oc + 1]),
                      reads=[bufof(f"ps{pj}"), bufof("consts")], writes=[R1("x1r", oc)],
                      note=tb + f"x1ev{oc}")
        if stage >= 1 and i + 1 < L // T:
            def exev(e):
                last = None
                for oc in range(NDC):
                    base = oc * (T + 2)
                    last = e.add(x1[:, base + T + 1:base + T + 2],
                                 ps_bcB[:, oc:oc + 1], csb["w1b"][:, oc:oc + 1])
                return last
            S.add("scalar", exev,
                  reads=[bufof("psB0"), bufof("consts")], writes=RX1, note=tb + "exev")
        elif stage >= 1:
            S.add("vector", lambda e: e.memset(
                      x1[:, :].rearrange("p (dc c) -> p dc c", dc=NDC)[:, :, T + 1:T + 2], 0.0),
                  writes=RX1, note=tb + "exz")
        ffn_steps = []

        def add_w12(hoc):
            j = (hoc // 4) % 2
            jj = 1 - j
            pj = (2 * hoc) % 3
            pj2 = (2 * hoc + 1) % 3
            s4 = hoc % 4
            if s4 == 0:
                wslab4("w1f", hoc // 4, j, tb + f"w1f{hoc}")
                wslab4("w2f", hoc // 4, jj, tb + f"w2f{hoc}")
            S.add("tensor", lambda e, j=j, s4=s4, pj=pj: mm_group(e, j, s4 * D, pj, xn_fm),
                  reads=[bufof(f"wffn{j}"), bufof("xn_fm")], writes=[bufof(f"ps{pj}")],
                  note=tb + f"w1mm{hoc}")
            S.add("scalar", lambda e, pj=pj, hoc=hoc: e.activation(
                      out=a_scr[hoc % 2][:, :], in_=ps_mm[pj][:, :], func=AF.Silu,
                      scale=csb["w1s"][:, hoc:hoc + 1], bias=csb["zerof"][:, 0:1]),
                  reads=[bufof(f"ps{pj}"), bufof("consts")], writes=[bufof(f"a_scr{hoc % 2}")],
                  note=tb + f"w1ev{hoc}")
            S.add("tensor", lambda e, jj=jj, s4=s4, pj2=pj2: mm_group(e, jj, s4 * D, pj2, xn_fm),
                  reads=[bufof(f"wffn{jj}"), bufof("xn_fm")], writes=[bufof(f"ps{pj2}")],
                  note=tb + f"w2mm{hoc}")
            S.add("vector", lambda e, pj2=pj2, hoc=hoc: e.scalar_tensor_tensor(
                      hA[:, hoc * T:(hoc + 1) * T], ps_mm[pj2][:, :],
                      csb["w2s"][:, hoc:hoc + 1], a_scr[hoc % 2][:, :], ALU.mult, ALU.mult),
                  reads=[bufof(f"ps{pj2}"), bufof(f"a_scr{hoc % 2}"), bufof("consts")],
                  writes=[R1("hA", hoc)], note=tb + f"w2ev{hoc}")

        def add_h(hoc, key, srcb, dstb, scol, note):
            j = hoc % 2
            pj = hoc % 3
            wslab(key, hoc, wffn, j, tb + f"{key}{hoc}")
            def hmm(e, j=j, pj=pj, srcb=srcb):
                last = None
                for k in range(NOC):
                    last = e.matmul(ps_mm[pj][:, :], wffn[j][:, k * 128:(k + 1) * 128],
                                    srcb[:, k * T:(k + 1) * T], start=(k == 0), stop=(k == NOC - 1))
                return last
            S.add("tensor", hmm, reads=[bufof(f"wffn{j}")] + (RHA if srcb is hA else RHB),
                  writes=[bufof(f"ps{pj}")], note=tb + f"{note}mm{hoc}")
            S.add("scalar", lambda e, pj=pj, hoc=hoc, scol=scol, dstb=dstb: e.activation(
                      out=dstb[:, hoc * T:(hoc + 1) * T], in_=ps_mm[pj][:, :], func=AF.Gelu,
                      scale=csb[scol][:, hoc:hoc + 1], bias=csb["zerof"][:, 0:1]),
                  reads=[bufof(f"ps{pj}"), bufof("consts")],
                  writes=[R1("hA" if dstb is hA else "hB", hoc)], note=tb + f"{note}ev{hoc}")

        def add_w3(oc3):
            j = oc3 % 2
            pj = oc3 % 3
            wslab("w3f", oc3, wffn, j, tb + f"w3f{oc3}")
            def w3mm(e, j=j, pj=pj):
                last = None
                for k in range(NOC):
                    last = e.matmul(ps_mm[pj][:, :], wffn[j][:, k * 128:(k + 1) * 128],
                                    hA[:, k * T:(k + 1) * T], start=(k == 0), stop=(k == NOC - 1))
                return last
            S.add("tensor", w3mm, reads=[bufof(f"wffn{j}")] + RHA,
                  writes=[bufof(f"ps{pj}")], note=tb + f"w3mm{oc3}")
            S.add("vector", lambda e, pj=pj, oc3=oc3: e.scalar_tensor_tensor(
                      fm(attn, oc3), ps_mm[pj][:, :], csb["w3s"][:, oc3:oc3 + 1],
                      fm(attn, oc3), ALU.mult, ALU.add),
                  reads=[bufof(f"ps{pj}"), bufof("consts"), R1("at", oc3)], writes=[R1("at", oc3)],
                  note=tb + f"w3ev{oc3}")

        n_early_w12 = 4 if stage >= 5 else 0
        for hoc in range(n_early_w12):
            add_w12(hoc)
        if stage < 2:
            continue
        S.add("vector", lambda e: e.tensor_copy(
                  x1[:, :].rearrange("p (dc c) -> p dc c", dc=NDC)[:, :, 0:1],
                  x1b[:, :].rearrange("p (dc c) -> p dc c", c=1)),
              reads=[bufof("x1b")], writes=RX1, note=tb + "brest")
        for p in range(NDC // 2):
            for dc in (2 * p, 2 * p + 1):
                base = dc * (T + 2)
                S.add("vector", lambda e, base=base, dc=dc: e.tensor_scalar_mul(
                          bS[dc % 2][:, :], x1[:, base:base + T], csb["convw"][:, 3 * dc:3 * dc + 1]),
                      reads=[R1("x1r", dc), bufof("consts")], writes=[bufof(f"bS{dc % 2}")],
                      note=tb + f"conv{dc}a")
            for dc in (2 * p, 2 * p + 1):
                base = dc * (T + 2)
                S.add("vector", lambda e, base=base, dc=dc: e.scalar_tensor_tensor(
                          bS[dc % 2][:, :], x1[:, base + 1:base + 1 + T],
                          csb["convw"][:, 3 * dc + 1:3 * dc + 2], bS[dc % 2][:, :], ALU.mult, ALU.add),
                      reads=[R1("x1r", dc), bufof("consts"), bufof(f"bS{dc % 2}")],
                      writes=[bufof(f"bS{dc % 2}")], note=tb + f"conv{dc}b")
            for dc in (2 * p, 2 * p + 1):
                base = dc * (T + 2)
                S.add("vector", lambda e, base=base, dc=dc: e.scalar_tensor_tensor(
                          bS[dc % 2][:, :], x1[:, base + 2:base + 2 + T],
                          csb["convw"][:, 3 * dc + 2:3 * dc + 3], bS[dc % 2][:, :], ALU.mult, ALU.add),
                      reads=[R1("x1r", dc), bufof("consts"), bufof(f"bS{dc % 2}")],
                      writes=[bufof(f"bS{dc % 2}")], note=tb + f"conv{dc}c")
            for dc in (2 * p, 2 * p + 1):
                S.add("scalar", lambda e, dc=dc: e.activation(out=fm(u_bf, dc), in_=bS[dc % 2][:, :],
                                                              func=AF.Silu),
                      reads=[bufof(f"bS{dc % 2}")], writes=[R1("u", dc)], note=tb + f"usilu{dc}")
        S.add("vector", lambda e: e.tensor_copy(
                  x1b[:, :].rearrange("p (dc c) -> p dc c", c=1),
                  x1[:, :].rearrange("p (dc c) -> p dc c", dc=NDC)[:, :, T:T + 1]),
              reads=RX1, writes=[bufof("x1b")], note=tb + "bsave")

        if stage < 3:
            continue
        for oc in range(NDC):
            j = (oc // 4) % 2
            if oc % 4 == 0:
                wslab4("wdtm", oc // 4, j, tb + f"wdtm{oc}")
            pj = oc % 3
            S.add("tensor", lambda e, j=j, oc=oc, pj=pj: mm_group(e, j, (oc % 4) * D, pj, u_bf),
                  reads=[bufof(f"wffn{j}")] + RU, writes=[bufof(f"ps{pj}")],
                  note=tb + f"dtmm{oc}")
            S.add("scalar", lambda e, pj=pj, oc=oc: e.activation(
                      out=s_scr[0][:, :], in_=ps_mm[pj][:, :], func=AF.Abs,
                      bias=csb["bdt"][:, oc:oc + 1], scale=csb["onesf"][:, 0:1]),
                  reads=[bufof(f"ps{pj}"), bufof("consts")], writes=[bufof("s_scr0")],
                  note=tb + f"sp{oc}abs")
            S.add("scalar", lambda e: e.activation(
                      out=s_scr[0][:, :], in_=s_scr[0][:, :], func=AF.Exp, scale=-1.0),
                  reads=[bufof("s_scr0")], writes=[bufof("s_scr0")], note=tb + f"sp{oc}exp")
            S.add("scalar", lambda e: e.activation(
                      out=s_scr[0][:, :], in_=s_scr[0][:, :], func=AF.Ln, bias=1.0),
                  reads=[bufof("s_scr0")], writes=[bufof("s_scr0")], note=tb + f"sp{oc}ln")
            S.add("scalar", lambda e, pj=pj, oc=oc: e.activation(
                      out=aS[0][:, :], in_=ps_mm[pj][:, :], func=AF.Relu,
                      bias=csb["bdt"][:, oc:oc + 1], scale=csb["onesf"][:, 0:1]),
                  reads=[bufof(f"ps{pj}"), bufof("consts")], writes=[bufof("aS0")],
                  note=tb + f"sp{oc}relu")
            S.add("vector", lambda e, oc=oc: e.tensor_add(fm(dt_bf, oc), s_scr[0][:, :], aS[0][:, :]),
                  reads=[bufof("s_scr0"), bufof("aS0")], writes=[R1("dt", oc)],
                  note=tb + f"dtadd{oc}")

        if stage < 4:
            continue
        def btmm(e, wsb, ps):
            last = None
            for dc in range(NDC):
                last = e.matmul(ps[0:16, :], wsb[:, dc * NST:(dc + 1) * NST], fm(u_bf, dc),
                                start=(dc == 0), stop=(dc == NDC - 1))
            return last
        S.add("tensor", lambda e: btmm(e, WBb, ps_bcB),
              reads=[bufof("WBb")] + RU, writes=[bufof("psB0")], note=tb + "btmm")
        S.add("scalar", lambda e: e.copy(Bt_sb[:, :], ps_bcB[0:16, :]),
              reads=[bufof("psB0")], writes=[bufof("Bt")], note=tb + "btev")

        S.add("tensor", lambda e: btmm(e, WCb, ps_bcC),
              reads=[bufof("WCb")] + RU, writes=[bufof("psC0")], note=tb + "ctmm")
        S.add("scalar", lambda e: e.copy(Ct_sb[:, :], ps_bcC[0:16, :]),
              reads=[bufof("psC0")], writes=[bufof("Ct")], note=tb + "ctev")


        for oc in range(NDC):
            j = (oc // 4) % 2
            if oc % 4 == 0:
                wslab4("v1m", oc // 4, j, tb + f"v1m{oc}")
            pj = oc % 3
            S.add("tensor", lambda e, j=j, oc=oc, pj=pj: mm_group(e, j, (oc % 4) * D, pj, xm_fm),
                  reads=[bufof(f"wffn{j}")] + RXM, writes=[bufof(f"ps{pj}")],
                  note=tb + f"vmm{oc}")
            S.add("scalar", lambda e, pj=pj, oc=oc: e.activation(
                      out=x1[:, oc * (T + 2) + 1: oc * (T + 2) + 1 + T], in_=ps_mm[pj][:, :],
                      func=AF.Silu, bias=csb["v1b"][:, oc:oc + 1], scale=csb["onesf"][:, 0:1]),
                  reads=[bufof(f"ps{pj}"), bufof("consts")], writes=[R1("x1r", oc)],
                  note=tb + f"vev{oc}")

        for dc in range(NDC):
            S.add("vector", lambda e, dc=dc: e.tensor_mul(fm(w_all, dc), fm(dt_bf, dc), fm(u_bf, dc)),
                  reads=[R1("dt", dc), R1("u", dc)], writes=[R1("wal", dc)], note=tb + f"w{dc}")
        S.add("vector", lambda e: e.memset(y_f[:, :], 0.0), writes=RY, note=tb + "y0")

        if stage < 5:
            continue
        for hoc in range(n_early_w12, NOC):
            ffn_steps.append(lambda hoc=hoc: add_w12(hoc))
        for hoc in range(NOC):
            ffn_steps.append(lambda hoc=hoc: add_h(hoc, "h1f", hA, hB, "h1s", "h1"))
        for hoc in range(NOC):
            ffn_steps.append(lambda hoc=hoc: add_h(hoc, "h2f", hB, hA, "h2s", "h2"))

        def add_scan(n):
            jn = n % 2
            S.add("tensor", lambda e, n=n, jn=jn: e.matmul(
                      ps_bcBd[jn][:, :], csb["idb"][0:16, n:n + 1].broadcast_to([16, 128]),
                      Bt_sb[:, :], start=True, stop=True),
                  reads=[bufof("Bt"), bufof("consts")], writes=[bufof(f"psB{jn}")], note=tb + f"bcB{n}")
            S.add("tensor", lambda e, n=n, jn=jn: e.matmul(
                      ps_bcCd[jn][:, :], csb["idb"][0:16, n:n + 1].broadcast_to([16, 128]),
                      Ct_sb[:, :], start=True, stop=True),
                  reads=[bufof("Ct"), bufof("consts")], writes=[bufof(f"psC{jn}")], note=tb + f"bcC{n}")
            for p in range(NDC // 2):
                dcs = (2 * p, 2 * p + 1)
                for dc in dcs:
                    ja = (n * NDC + dc) % 4
                    S.add("scalar", lambda e, n=n, dc=dc, ja=ja: e.activation(
                              out=aS[ja][:, :], in_=fm(dt_bf, dc), func=AF.Exp,
                              scale=A_sb[:, dc * NST + n:dc * NST + n + 1], bias=csb["zerof"][:, 0:1]),
                          reads=[R1("dt", dc), bufof("A")], writes=[bufof(f"aS{ja}")],
                          note=tb + f"dA{n}.{dc}")
                for dc in dcs:
                    S.add("vector", lambda e, dc=dc, jn=jn: e.tensor_mul(
                              bS[dc % 2][:, :], fm(w_all, dc), ps_bcBd[jn][:, :]),
                          reads=[R1("wal", dc), bufof(f"psB{jn}")], writes=[bufof(f"bS{dc % 2}")],
                          note=tb + f"bmul{n}.{dc}")
                for dc in dcs:
                    ja = (n * NDC + dc) % 4
                    S.add("vector", lambda e, n=n, dc=dc, ja=ja: e.tensor_tensor_scan(
                              hS[dc % 2][:, :], aS[ja][:, :], bS[dc % 2][:, :],
                              hstate[:, dc * NST + n:dc * NST + n + 1], ALU.mult, ALU.add),
                          reads=[bufof(f"aS{ja}"), bufof(f"bS{dc % 2}"), R1("hst", dc)],
                          writes=[bufof(f"hS{dc % 2}")], note=tb + f"scan{n}.{dc}")
                for dc in dcs:
                    S.add("vector", lambda e, dc=dc, jn=jn: e.tensor_mul(
                              bS[dc % 2][:, :], hS[dc % 2][:, :], ps_bcCd[jn][:, :]),
                          reads=[bufof(f"hS{dc % 2}"), bufof(f"psC{jn}")], writes=[bufof(f"bS{dc % 2}")],
                          note=tb + f"ymul{n}.{dc}")
                for dc in dcs:
                    S.add("vector", lambda e, n=n, dc=dc: e.tensor_copy(
                              hstate[:, dc * NST + n:dc * NST + n + 1], hS[dc % 2][:, T - 1:T]),
                          reads=[bufof(f"hS{dc % 2}")], writes=[R1("hst", dc)],
                          note=tb + f"hsv{n}.{dc}")
                for dc in dcs:
                    S.add("gpsimd", lambda e, dc=dc: e.tensor_add(
                              fm(y_f, dc), fm(y_f, dc), bS[dc % 2][:, :]),
                          reads=[R1("y", dc), bufof(f"bS{dc % 2}")], writes=[R1("y", dc)],
                          note=tb + f"yacc{n}.{dc}", dma=False)

        fi = 0
        front_done = False
        for n in range(NST):
            add_scan(n)
            take = 6 if n < NST - 1 else len(ffn_steps) - fi
            for _ in range(take):
                if fi < len(ffn_steps):
                    ffn_steps[fi]()
                    fi += 1
                    if fi == 40 and not front_done and i + 1 < n_tiles:
                        emit_front(i + 1)
                        front_done = True
        while fi < len(ffn_steps):
            ffn_steps[fi]()
            fi += 1
        if not front_done and i + 1 < n_tiles:
            emit_front(i + 1)

        if stage < 6:
            continue
        for dc in range(NDC):
            S.add("vector", lambda e, dc=dc: e.scalar_tensor_tensor(
                      fm(y_f, dc), fm(u_bf, dc), csb["dsk"][:, dc:dc + 1],
                      fm(y_f, dc), ALU.mult, ALU.add),
                  reads=[R1("u", dc), bufof("consts"), R1("y", dc)],
                  writes=[R1("y", dc)], note=tb + f"gate{dc}a")
        for dc in range(NDC):
            S.add("vector", lambda e, dc=dc: e.tensor_mul(
                      fm(u_bf, dc), fm(y_f, dc),
                      x1[:, dc * (T + 2) + 1: dc * (T + 2) + 1 + T]),
                  reads=[R1("y", dc), R1("x1r", dc)],
                  writes=[R1("u", dc)], note=tb + f"gate{dc}b")
        for oc in range(NDC):
            j = (oc // 4) % 2
            if oc % 4 == 0:
                wslab4("w2m", oc // 4, j, tb + f"w2m{oc}")
            pj = oc % 3
            S.add("tensor", lambda e, j=j, oc=oc, pj=pj: mm_group(e, j, (oc % 4) * D, pj, u_bf),
                  reads=[bufof(f"wffn{j}")] + RU, writes=[bufof(f"ps{pj}")],
                  note=tb + f"w2amm{oc}")
            S.add("scalar", lambda e, pj=pj, oc=oc: e.add(
                      fm(attn, oc), ps_mm[pj][:, :], csb["w2b"][:, oc:oc + 1]),
                  reads=[bufof(f"ps{pj}"), bufof("consts")], writes=[R1("at", oc)],
                  note=tb + f"w2aev{oc}")

        if stage < 7:
            continue
        for oc3 in range(NDC):
            add_w3(oc3)

        if stage < 8:
            continue
        for tc in range(NTC):
            S.add("gpsimd",
                  lambda e, i=i, tc=tc: e.dma_start(
                      out=xtl[tc % 2][:, :],
                      in_=x_d[i * T + tc * 128: i * T + (tc + 1) * 128, :]),
                  reads=[], writes=[bufof(f"xtl{tc % 2}")], note=tb + f"xtload{tc}")
            def trS(e, tc=tc):
                last = None
                for dc in range(NDC):
                    last = e.transpose(ps_tail[:, dc * 128:(dc + 1) * 128],
                                       attn[:, dc * T + tc * 128: dc * T + (tc + 1) * 128],
                                       csb["idb"][:, :])
                return last
            S.add("tensor", trS, reads=RAT + [bufof("consts")], writes=[bufof("psT")],
                  note=tb + f"trS{tc}")
            jo = tc % 2
            S.add("vector", lambda e, tc=tc, jo=jo: e.tensor_add(
                      out_sb[jo][:, :], ps_tail[:, :], xtl[tc % 2][:, :]),
                  reads=[bufof("psT"), bufof(f"xtl{tc % 2}")], writes=[bufof(f"out_sb{jo}")],
                  note=tb + f"oadd{tc}")
            S.add("gpsimd", lambda e, i=i, tc=tc, jo=jo: e.dma_start(
                      out=out_d[i * T + tc * 128: i * T + (tc + 1) * 128, :], in_=out_sb[jo][:, :]),
                  reads=[bufof(f"out_sb{jo}")], writes=[], note=tb + f"ostore{tc}")

    dbg_map = dict(xn_fm=xn_fm, xm_fm=xm_fm, x1=x1, u_bf=u_bf, dt_bf=dt_bf,
                   y_f=y_f, attn=attn, hA=hA, hB=hB, w_all=w_all,
                   xn_tm=xn_tm, ex_st=ex_st, out_sb1=out_sb[1], x_tm=xfr[0])
    for name in debug_taps:
        buf = dbg_map[name]
        shp = list(buf.shape)
        dt = buf.dtype
        od = nc.dram_tensor("dbg_" + name, shp, dt, kind="ExternalOutput").ap()
        S.add("gpsimd", lambda e, od=od, buf=buf: e.dma_start(out=od[:, :], in_=buf[:, :]),
              reads=[bufof({"xn_fm": "xn_fm", "xm_fm": "xmf0", "x1": "x1r0", "u_bf": "u0",
                            "dt_bf": "dt0", "y_f": "y0", "attn": "at0", "hA": "hA0",
                            "hB": "hB0", "w_all": "wal0", "x_tm": "xfr0", "xn_tm": "xn_tm",
                            "ex_st": "ex_st", "out_sb1": "out_sb1"}[name])],
              note=f"dbg {name}")

    S.finalize()
    sems = {}
    semctx = []
    semnames = list(ENGINES) + [e + str(j) for e, k in _DMA_RING.items() for j in range(k)]
    for e in semnames:
        h = nc.semaphore("sem_" + e)
        semctx.append(h)
        sems[e] = h.__enter__()
    blk = nc.Block()
    block = blk.__enter__()
    for ename in ENGINES:
        def body(eng, ename=ename):
            S.emit_engine(ename, eng, sems)
            if ename in _DMA_RING:
                for j in range(_DMA_RING[ename]):
                    nm = ename + str(j)
                    if S.totals.get(nm):
                        eng.wait_ge(sems[nm], S.totals[nm])
        getattr(block, ename)(body)
    blk.__exit__(None, None, None)
    for h in reversed(semctx):
        h.__exit__(None, None, None)
    for h in reversed(ctx):
        h.__exit__(None, None, None)
    return nc


_CACHE = {}


def _get_program():
    if "nc" not in _CACHE:
        _CACHE["nc"] = build_program()
    return _CACHE["nc"]


def kernel(**inputs):
    nc = _get_program()
    hp = host_prep(inputs)
    x = np.asarray(inputs["L0_attn_x"], np.float32)
    in_maps = []
    for b in range(8):
        m = dict(hp)
        m["x"] = np.ascontiguousarray(x[b])
        in_maps.append(m)
    res = run_bass_kernel_spmd(nc, in_maps, core_ids=list(range(8)))
    return np.stack([r["out"] for r in res.results], axis=0)


# version 09
# speedup vs baseline: 1.1663x; 1.1300x over previous
import numpy as np
import ml_dtypes
import concourse.bass as bass
import concourse.mybir as mybir
from concourse.bass_utils import run_bass_kernel_spmd

F32 = mybir.dt.float32
BF16 = mybir.dt.bfloat16
AF = mybir.ActivationFunctionType
ALU = mybir.AluOpType
AX = mybir.AxisListType

L, D, NST, HID = 8192, 1024, 16, 4096
T = 512
NTC = T // 128
NDC = D // 128
NOC = HID // 128
EPS = 1e-5
RD = 1.0 / D

ENGINES = ("sync", "gpsimd", "tensor", "vector", "scalar")
_UNIT = {"sync": 16, "gpsimd": 16, "tensor": 1, "vector": 1, "scalar": 1}
_DMA_RING = {"sync": 6, "gpsimd": 3}


class Buf:
    __slots__ = ("name", "last_w", "readers")

    def __init__(self, name):
        self.name = name
        self.last_w = None
        self.readers = []


class Sched:
    def __init__(self):
        self.steps = []

    def add(self, eng, emit, reads=(), writes=(), note="", dma=None):
        if dma is None:
            dma = eng in ("sync", "gpsimd")
        sid = len(self.steps)
        deps = set()
        for b in reads:
            if b.last_w is not None:
                deps.add(b.last_w)
        for b in writes:
            if b.last_w is not None:
                deps.add(b.last_w)
            for r in b.readers:
                deps.add(r)
        self.steps.append(dict(id=sid, eng=eng, emit=emit, deps=deps, dma=dma,
                               note=note, needed=False, cum=0))
        for b in reads:
            b.readers.append(sid)
        for b in writes:
            b.last_w = sid
            b.readers = []
        return sid

    def finalize(self):
        steps = self.steps
        eng_of = [s["eng"] for s in steps]
        for s in steps:
            for d in s["deps"]:
                if eng_of[d] != s["eng"] or steps[d]["dma"]:
                    steps[d]["needed"] = True
        for s in steps:
            if s["dma"]:
                s["needed"] = True
        cum = {}
        eseq = {e: 0 for e in ENGINES}
        dma_ct = {e: 0 for e in _DMA_RING}
        for s in steps:
            if s["dma"]:
                j = dma_ct[s["eng"]]
                dma_ct[s["eng"]] += 1
                s["sem"] = s["eng"] + str(j % _DMA_RING[s["eng"]])
            else:
                s["sem"] = s["eng"]
            if s["needed"]:
                cum[s["sem"]] = cum.get(s["sem"], 0) + _UNIT[s["eng"]]
            s["cum"] = cum.get(s["sem"], 0)
            s["eseq"] = eseq[s["eng"]]
            eseq[s["eng"]] += 1
        self.totals = dict(cum)

    def emit_engine(self, ename, eng, sems):
        steps = self.steps
        guar = {}
        drains_ok = ename in ("vector", "scalar", "gpsimd")
        drained_upto = 0
        n = 0
        for s in steps:
            if s["eng"] != ename:
                continue
            need = {}
            same_need_drain = False
            for d in s["deps"]:
                de = steps[d]["eng"]
                if de == ename and not steps[d]["dma"]:
                    if (drains_ok and not s["dma"] and steps[d]["eseq"] >= drained_upto
                            and steps[d]["eseq"] >= s["eseq"] - 1):
                        same_need_drain = True
                    continue
                assert steps[d]["needed"], (s["note"], steps[d]["note"])
                sm = steps[d]["sem"]
                t = steps[d]["cum"]
                if t > guar.get(sm, 0) and t > need.get(sm, 0):
                    need[sm] = t
            if same_need_drain:
                eng.drain()
                drained_upto = s["eseq"]
            for sm in sorted(need):
                eng.wait_ge(sems[sm], need[sm])
                guar[sm] = need[sm]
            last = s["emit"](eng)
            if s["needed"]:
                assert last is not None, f"step {s['id']} {s['note']} returned no instruction"
                last.then_inc(sems[s["sem"]], _UNIT[ename])
            n += 1
        return n


def host_prep(inputs):
    g = lambda k: np.asarray(inputs[k], dtype=np.float32)

    def prep_mix(w):
        return np.ascontiguousarray(
            w.reshape(NDC, 128, NDC, 128).transpose(2, 1, 0, 3).reshape(NDC, 128, D))

    def prep_T(w):
        O, I = w.shape
        no, ni = O // 128, I // 128
        return np.ascontiguousarray(
            w.reshape(no, 128, ni, 128).transpose(0, 3, 2, 1).reshape(no, 128, I))

    def cols(v):
        return np.ascontiguousarray(np.asarray(v, np.float32).reshape(-1, 128).T)

    d = {}
    d["w1m"] = prep_mix(g("L0_attn_w1_w"))
    d["v1m"] = prep_mix(g("L0_attn_v1_w"))
    d["wdtm"] = prep_mix(g("L0_attn_W_dt"))
    d["w2m"] = prep_mix(g("L0_attn_w2_w"))
    d["w1f"] = prep_T(g("L0_ffn_w1_w"))
    d["w2f"] = prep_T(g("L0_ffn_w2_w"))
    d["h1f"] = prep_T(g("L0_ffn_h1_w"))
    d["h2f"] = prep_T(g("L0_ffn_h2_w"))
    d["w3f"] = prep_T(g("L0_ffn_w3_w"))
    d["alog"] = np.ascontiguousarray(
        g("L0_attn_A_log").reshape(NDC, 128, NST).transpose(1, 0, 2).reshape(128, NDC * NST))
    d["convw"] = np.ascontiguousarray(
        g("L0_attn_conv_w").reshape(D, 3).reshape(NDC, 128, 3).transpose(1, 0, 2).reshape(128, NDC * 3))
    d["dsk"] = cols(g("L0_attn_D_skip"))
    d["w1b"] = cols(g("L0_attn_w1_b"))
    d["v1b"] = cols(g("L0_attn_v1_b"))
    d["w2b"] = cols(g("L0_attn_w2_b"))
    d["bdt"] = cols(g("L0_attn_b_dt"))
    d["w1s"] = cols(g("L0_ffn_w1_s"))
    d["w2s"] = cols(g("L0_ffn_w2_s"))
    d["h1s"] = cols(g("L0_ffn_h1_s"))
    d["h2s"] = cols(g("L0_ffn_h2_s"))
    d["w3s"] = cols(g("L0_ffn_w3_s"))
    d["wb"] = np.ascontiguousarray(
        g("L0_attn_W_B").reshape(NDC, 128, NST).transpose(1, 0, 2).reshape(128, NDC * NST))
    d["wc"] = np.ascontiguousarray(
        g("L0_attn_W_C").reshape(NDC, 128, NST).transpose(1, 0, 2).reshape(128, NDC * NST))
    d["gcol"] = cols(g("L0_attn_ln_g"))
    d["bcol"] = cols(g("L0_attn_ln_b"))
    d["onesf"] = np.ones((128, 1), np.float32)
    d["zerof"] = np.zeros((128, 1), np.float32)
    d["epsf"] = np.full((128, 1), 1e-5, np.float32)
    d["onecol"] = np.ones((1, 1), ml_dtypes.bfloat16)
    d["idb"] = np.eye(128, dtype=ml_dtypes.bfloat16)
    return d


def build_program(n_tiles=L // T, debug_taps=(), stage=99):
    nc = bass.Bass("TRN2", target_bir_lowering=False)

    x_d = nc.dram_tensor("x", [L, D], F32, kind="ExternalInput").ap()
    out_d = nc.dram_tensor("out", [L, D], F32, kind="ExternalOutput").ap()

    wspec = dict(w1m=(NDC, 128, D), v1m=(NDC, 128, D), wdtm=(NDC, 128, D),
                 w2m=(NDC, 128, D), w1f=(NOC, 128, D), w2f=(NOC, 128, D),
                 h1f=(NOC, 128, HID), h2f=(NOC, 128, HID), w3f=(NDC, 128, HID))
    wd = {}
    ws = {}
    for k, shp in wspec.items():
        wd[k] = nc.dram_tensor(k, list(shp), F32, kind="ExternalInput").ap()
        ws[k] = nc.dram_tensor(k + "S", list(shp), BF16, kind="Internal").ap()

    cspec_f32 = dict(alog=(128, NDC * NST), convw=(128, NDC * 3), dsk=(128, NDC),
                     w1b=(128, NDC), v1b=(128, NDC), w2b=(128, NDC), bdt=(128, NDC),
                     w1s=(128, NOC), w2s=(128, NOC), h1s=(128, NOC), h2s=(128, NOC),
                     w3s=(128, NDC), gcol=(128, NDC), bcol=(128, NDC))
    cd = {k: nc.dram_tensor(k, list(s), F32, kind="ExternalInput").ap()
          for k, s in cspec_f32.items()}
    cd["wb"] = nc.dram_tensor("wb", [128, NDC * NST], F32, kind="ExternalInput").ap()
    cd["wc"] = nc.dram_tensor("wc", [128, NDC * NST], F32, kind="ExternalInput").ap()

    cd["onecol"] = nc.dram_tensor("onecol", [1, 1], BF16, kind="ExternalInput").ap()
    cd["idb"] = nc.dram_tensor("idb", [128, 128], BF16, kind="ExternalInput").ap()
    cd["onesf"] = nc.dram_tensor("onesf", [128, 1], F32, kind="ExternalInput").ap()
    cd["zerof"] = nc.dram_tensor("zerof", [128, 1], F32, kind="ExternalInput").ap()
    cd["epsf"] = nc.dram_tensor("epsf", [128, 1], F32, kind="ExternalInput").ap()

    ctx = []

    def sb(name, shape, dt):
        h = nc.sbuf_tensor(name, list(shape), dt)
        ctx.append(h)
        return h.__enter__()

    def psum(name, shape, dt):
        h = nc.psum_tensor(name, list(shape), dt)
        ctx.append(h)
        return h.__enter__()

    xfr = [sb(f"xfr{j}", [128, D], F32) for j in range(2)]
    xtl = [sb(f"xtl{j}", [128, D], F32) for j in range(2)]
    xn_tm = sb("xn_tm", [128, D], BF16)
    xm_tm = xn_tm
    stats = sb("stats", [128, 32], F32)
    xn_fm = sb("xn_fm", [128, NDC * T], BF16)
    xm_fm = sb("xm_fm", [128, NDC * T], BF16)
    x1 = sb("x1", [128, NDC * (T + 2)], BF16)
    u_bf = sb("u_bf", [128, NDC * T], BF16)
    dt_bf = sb("dt_bf", [128, NDC * T], BF16)
    w_all = sb("w_all", [128, NDC * T], BF16)
    y_f = sb("y_f", [128, NDC * T], F32)
    attn = sb("attn", [128, NDC * T], BF16)
    hA = sb("hA", [128, NOC * T], BF16)
    hB = sb("hB", [128, NOC * T], BF16)
    aS = [sb(f"aS{j}", [128, T], F32) for j in range(4)]
    bS = [sb(f"bS{j}", [128, T], F32) for j in range(2)]
    hS = [sb(f"hS{j}", [128, T], F32) for j in range(2)]
    hstate = sb("hstate", [128, NDC * NST], F32)
    x1b = sb("x1b", [128, NDC], BF16)
    xcol = sb("xcol", [128, NDC], BF16)
    Bt_sb = sb("Bt_sb", [16, T], BF16)
    Ct_sb = sb("Ct_sb", [16, T], BF16)
    a_scr = [sb(f"a_scr{j}", [128, T], BF16) for j in range(2)]
    wffn = [sb(f"wffn{j}", [128, HID], BF16) for j in range(2)]
    wmix = wffn
    out_sb = [sb(f"out_sb{j}", [128, D], F32) for j in range(2)]
    s_scr = [sb("s_scr0", [128, T], F32)]
    x_ex = out_sb[1][0:1, :]
    xn_ex = x_ex
    xm_ex = xm_tm
    ex_st = sb("ex_st", [1, 32], F32)
    WBb = sb("WBb", [128, NDC * NST], BF16)
    WCb = sb("WCb", [128, NDC * NST], BF16)
    csb = {}
    for k, s in cspec_f32.items():
        csb[k] = sb("c_" + k, list(s), F32)
    csb["onecol"] = sb("c_onecol", [1, 1], BF16)
    csb["idb"] = sb("c_idb", [128, 128], BF16)
    csb["onesf"] = sb("c_onesf", [128, 1], F32)
    csb["zerof"] = sb("c_zerof", [128, 1], F32)
    csb["epsf"] = sb("c_epsf", [128, 1], F32)

    ps_mm = [psum(f"ps_mm{j}", [128, T], F32) for j in range(3)]
    ps_bcBd = [psum(f"ps_bcB{j}", [128, T], F32) for j in range(2)]
    ps_bcCd = [psum(f"ps_bcC{j}", [128, T], F32) for j in range(2)]
    ps_bcB = ps_bcBd[0]
    ps_bcC = ps_bcCd[0]
    ps_tail = psum("ps_tail", [128, D], BF16)

    B = {}

    def bufof(name):
        if name not in B:
            B[name] = Buf(name)
        return B[name]

    bufof("xcol")
    for nm in ("xn_tm xm_tm stats xn_fm x1b Bt Ct x_ex xm_ex ex_st "
               "WBb WCb consts psB psC psT xfr0 xfr1 xtl0 xtl1").split():
        bufof(nm)

    def R(base, n):
        return [bufof(f"{base}{i}") for i in range(n)]

    def R1(base, i):
        return bufof(f"{base}{i}")
    RY, RDT, RW, RU, RX1, RXM, RAT, RHS = (R("y", NDC), R("dt", NDC), R("wal", NDC),
                                           R("u", NDC), R("x1r", NDC), R("xmf", NDC),
                                           R("at", NDC), R("hst", NDC))
    RHA, RHB = R("hA", NOC), R("hB", NOC)
    for j in range(2):
        for nm in ("a_scr", "out_sb", "wffn", "bS", "hS"):
            bufof(f"{nm}{j}")
    for j in range(4):
        bufof(f"aS{j}")
    bufof("s_scr0")
    for j in range(3):
        bufof(f"ps{j}")
    for j in range(2):
        bufof(f"psB{j}")
        bufof(f"psC{j}")
    for k in wspec:
        bufof("scr_" + k)

    S = Sched()

    def c_load(k):
        S.add("sync", lambda e, k=k: e.dma_start(out=csb[k][:, :], in_=cd[k][:, :]),
              writes=[bufof("consts")], note=f"load {k}")
    for k in list(cspec_f32) + ["onecol", "idb", "onesf", "zerof", "epsf"]:
        c_load(k)
    for k in wspec:
        S.add("gpsimd", lambda e, k=k: e.dma_start(out=ws[k][:, :, :], in_=wd[k][:, :, :]),
              writes=[bufof("scr_" + k)], note=f"cast {k}")
    A_sb = csb["alog"]
    bufof("A")
    S.add("scalar", lambda e: e.activation(out=A_sb[:, :], in_=csb["alog"][:, :], func=AF.Exp),
          reads=[bufof("consts")], writes=[bufof("A")], note="expA")
    S.add("vector", lambda e: e.tensor_scalar_mul(A_sb[:, :], A_sb[:, :], -1.0),
          reads=[bufof("A")], writes=[bufof("A")], note="negA")
    S.add("gpsimd", lambda e: e.dma_start(out=WBb[:, :], in_=cd["wb"][:, :]),
          writes=[bufof("WBb")], note="WBb")
    S.add("gpsimd", lambda e: e.dma_start(out=WCb[:, :], in_=cd["wc"][:, :]),
          writes=[bufof("WCb")], note="WCb")
    S.add("vector", lambda e: e.memset(hstate[:, :], 0.0), writes=[bufof("hstate")], note="hs0")
    S.add("vector", lambda e: e.memset(x1b[:, :], 0.0), writes=[bufof("x1b")], note="x1b0")

    def fm(bufh, dc, lo=0, hi=T):
        return bufh[:, dc * T + lo: dc * T + hi]

    ACC = dict(mult=ALU.mult, add=ALU.add, subtract=ALU.subtract)

    def emit_front(i):
        tb = f"t{i}."
        if i + 1 < L // T:
            S.add("gpsimd",
                  lambda e, i=i: e.dma_start(out=x_ex, in_=x_d[(i + 1) * T:(i + 1) * T + 1, :]),
                  writes=[bufof("out_sb1")], note=tb + "xex")

        for tc in range(NTC):
            S.add("gpsimd",
                  lambda e, i=i, tc=tc: e.dma_start(
                      out=xfr[tc % 2][:, :],
                      in_=x_d[i * T + tc * 128: i * T + (tc + 1) * 128, :]),
                  reads=[], writes=[bufof(f"xfr{tc % 2}")], note=tb + f"xload{tc}")
            xch = xfr[tc % 2][:, :]
            xbuf = bufof(f"xfr{tc % 2}")
            st = stats

            def bn_steps(srcap, srcbuf, mcol, nt, npart=128):
                S.add("vector", lambda e: e.bn_stats(st[0:npart, 16:22], srcap[:, 0:512]),
                      reads=[srcbuf], writes=[bufof("stats")], note=nt + "bnA")
                S.add("vector", lambda e: e.bn_stats(st[0:npart, 22:28], srcap[:, 512:1024]),
                      reads=[srcbuf], writes=[bufof("stats")], note=nt + "bnB")
                S.add("vector", lambda e: e.bn_aggr(st[0:npart, mcol:mcol + 2], st[0:npart, 16:28]),
                      reads=[bufof("stats")], writes=[bufof("stats")], note=nt + "bnG")

            def rstd_steps(s, v, t, r, sbuf, nt):
                npart = s.shape[0]
                S.add("scalar", lambda e: e.activation(out=s[:, t:t + 1], in_=s[:, v:v + 1], func=AF.Ln,
                                                       bias=csb["epsf"][0:npart, 0:1],
                                                       scale=csb["onesf"][0:npart, 0:1]),
                      reads=[sbuf], writes=[sbuf], note=nt + "ln")
                S.add("scalar", lambda e: e.activation(out=s[:, r:r + 1], in_=s[:, t:t + 1],
                                                       func=AF.Exp, scale=-0.5),
                      reads=[sbuf], writes=[sbuf], note=nt + "exp")
            bn_steps(xch, xbuf, 2, tb + "v1.")
            rstd_steps(st, 3, 4, 5, bufof("stats"), tb + "r1.")
            S.add("vector", lambda e, xch=xch, s=st: e.tensor_scalar(
                      xn_tm[:, :], xch, s[:, 2:3], s[:, 5:6], ALU.subtract, ALU.mult),
                  reads=[xbuf, bufof("stats")], writes=[bufof("xn_tm")], note=tb + "xn")
            bn_steps(xn_tm[:, :], bufof("xn_tm"), 10, tb + "v2.")
            rstd_steps(st, 11, 12, 13, bufof("stats"), tb + "r2.")
            def tr(e, src):
                last = None
                for dc in range(NDC):
                    last = e.transpose(ps_tail[:, dc * 128:(dc + 1) * 128],
                                       src[:, dc * 128:(dc + 1) * 128], csb["idb"][:, :])
                return last
            def ev(e, dst, tcv):
                src3 = ps_tail[:, :].rearrange("p (dc t) -> p dc t", dc=NDC)
                dst3 = dst[:, :].rearrange("p (dc t) -> p dc t", dc=NDC)[:, :, tcv * 128:(tcv + 1) * 128]
                return e.copy(dst3, src3)
            S.add("tensor", lambda e, tr=tr: tr(e, xn_tm),
                  reads=[bufof("xn_tm"), bufof("consts")], writes=[bufof("psT")], note=tb + "trxn")
            S.add("scalar", lambda e, ev=ev, tc=tc: ev(e, xn_fm, tc),
                  reads=[bufof("psT")], writes=[bufof("xn_fm")], note=tb + "evxn")
            S.add("vector", lambda e, s=st: e.tensor_scalar(
                      xm_tm[:, :], xn_tm[:, :], s[:, 10:11], s[:, 13:14],
                      ALU.subtract, ALU.mult),
                  reads=[bufof("xn_tm"), bufof("stats")],
                  writes=[bufof("xn_tm")], note=tb + "xm1")
            S.add("tensor", lambda e, tr=tr: tr(e, xm_tm),
                  reads=[bufof("xn_tm"), bufof("consts")], writes=[bufof("psT")], note=tb + "trxm")
            def evxm(e, tcv):
                src3 = ps_tail[:, :].rearrange("p (dc t) -> p dc t", dc=NDC)
                last = None
                for dc in range(NDC):
                    last = e.tensor_scalar(
                        xm_fm[:, dc * T + tcv * 128: dc * T + (tcv + 1) * 128],
                        src3[:, dc, :], csb["gcol"][:, dc:dc + 1], csb["bcol"][:, dc:dc + 1],
                        ALU.mult, ALU.add)
                return last
            S.add("vector", lambda e, evxm=evxm, tc=tc: evxm(e, tc),
                  reads=[bufof("psT"), bufof("consts")], writes=RXM, note=tb + "evxm")

        if i + 1 < L // T:
            es = ex_st
            def exbn_steps(srcap, srcbuf, mcol, nt):
                S.add("vector", lambda e: e.bn_stats(es[:, 16:22], srcap[:, 0:512]),
                      reads=[srcbuf], writes=[bufof("ex_st")], note=nt + "bnA")
                S.add("vector", lambda e: e.bn_stats(es[:, 22:28], srcap[:, 512:1024]),
                      reads=[srcbuf], writes=[bufof("ex_st")], note=nt + "bnB")
                S.add("vector", lambda e: e.bn_aggr(es[:, mcol:mcol + 2], es[:, 16:28]),
                      reads=[bufof("ex_st")], writes=[bufof("ex_st")], note=nt + "bnG")
            exbn_steps(x_ex, bufof("out_sb1"), 2, tb + "exv1.")
            rstd_steps(es, 3, 4, 5, bufof("ex_st"), tb + "exr1.")
            S.add("vector", lambda e: e.tensor_scalar(
                      xn_ex, x_ex, es[:, 2:3], es[:, 5:6], ALU.subtract, ALU.mult),
                  reads=[bufof("out_sb1"), bufof("ex_st")], writes=[bufof("out_sb1")], note=tb + "exxn")
            exbn_steps(xn_ex, bufof("out_sb1"), 10, tb + "exv2.")
            rstd_steps(es, 11, 12, 13, bufof("ex_st"), tb + "exr2.")
            S.add("vector", lambda e: e.tensor_scalar(
                      xn_ex, xn_ex, es[:, 10:11], es[:, 13:14], ALU.subtract, ALU.mult),
                  reads=[bufof("out_sb1"), bufof("ex_st")],
                  writes=[bufof("out_sb1")], note=tb + "exxm1")
            S.add("vector", lambda e: e.tensor_copy(xm_ex[0:1, :], xn_ex),
                  reads=[bufof("out_sb1")], writes=[bufof("xn_tm")], note=tb + "exxm2")
            def exmm(e):
                last = None
                for dc in range(NDC):
                    last = e.matmul(ps_bcC[:, dc:dc + 1], xm_ex[0:1, dc * 128:(dc + 1) * 128],
                                    csb["onecol"][0:1, 0:1], start=True, stop=True)
                return last
            S.add("tensor", exmm, reads=[bufof("xn_tm"), bufof("consts")],
                  writes=[bufof("psC0")], note=tb + "exmm")
            def excol(e):
                last = None
                for dc in range(NDC):
                    last = e.tensor_scalar(xcol[:, dc:dc + 1], ps_bcC[:, dc:dc + 1],
                                           csb["gcol"][:, dc:dc + 1], csb["bcol"][:, dc:dc + 1],
                                           ALU.mult, ALU.add)
                return last
            S.add("vector", excol,
                  reads=[bufof("psC0"), bufof("consts")], writes=[bufof("xcol")], note=tb + "excol")


    for i in range(n_tiles):
        tb = f"t{i}."
        if i == 0:
            emit_front(0)
        def wslab(key, idx, bufset, j, nslab_note):
            S.add("sync",
                  lambda e, key=key, idx=idx, j=j: e.dma_start(out=wffn[j][:, 0:ws[key].shape[2]],
                                                               in_=ws[key][idx, :, :]),
                  reads=[bufof("scr_" + key)], writes=[bufof(f"wffn{j}")],
                  note=nslab_note)

        def wslab4(key, g, j, nslab_note):
            S.add("sync",
                  lambda e, key=key, g=g, j=j: e.dma_start(
                      out=wffn[j][:, :].rearrange("p (s f) -> p s f", s=4),
                      in_=ws[key][4 * g:4 * g + 4, :, :].rearrange("s p f -> p s f")),
                  reads=[bufof("scr_" + key)], writes=[bufof(f"wffn{j}")],
                  note=nslab_note)

        def mm_group(e, j, soff, pj, src, nk=NDC):
            last = None
            for k in range(nk):
                last = e.matmul(ps_mm[pj][:, :], wffn[j][:, soff + k * 128: soff + (k + 1) * 128],
                                fm(src, k), start=(k == 0), stop=(k == nk - 1))
            return last

        has_ex = stage >= 1 and i + 1 < L // T
        if stage >= 2:
            S.add("vector", lambda e: e.tensor_copy(
                      x1[:, :].rearrange("p (dc c) -> p dc c", dc=NDC)[:, :, 0:1],
                      x1b[:, :].rearrange("p (dc c) -> p dc c", c=1)),
                  reads=[bufof("x1b")], writes=RX1, note=tb + "brest")
        for p in range(4 if stage >= 1 else 0):
            for oc in (2 * p, 2 * p + 1):
                j = (oc // 4) % 2
                if oc % 4 == 0:
                    wslab4("w1m", oc // 4, j, tb + f"w1m{oc}")
                pj = oc % 3
                S.add("tensor", lambda e, j=j, oc=oc, pj=pj: mm_group(e, j, (oc % 4) * D, pj, xm_fm),
                      reads=[bufof(f"wffn{j}")] + RXM,
                      writes=[bufof(f"ps{pj}")], note=tb + f"x1mm{oc}")
                if has_ex:
                    def exg(e, j=j, oc=oc):
                        last = None
                        for k in range(NDC):
                            last = e.matmul(ps_bcB[:, oc:oc + 1],
                                            wffn[j][:, (oc % 4) * D + k * 128: (oc % 4) * D + (k + 1) * 128],
                                            xcol[:, k:k + 1], start=(k == 0), stop=(k == NDC - 1))
                        return last
                    S.add("tensor", exg, reads=[bufof(f"wffn{j}"), bufof("xcol")],
                          writes=[bufof("psB0")], note=tb + f"x1exmm{oc}")
                S.add("scalar", lambda e, pj=pj, oc=oc: e.add(
                          x1[:, oc * (T + 2) + 1: oc * (T + 2) + 1 + T], ps_mm[pj][:, :],
                          csb["w1b"][:, oc:oc + 1]),
                      reads=[bufof(f"ps{pj}"), bufof("consts")], writes=[R1("x1r", oc)],
                      note=tb + f"x1ev{oc}")
                if has_ex:
                    S.add("scalar", lambda e, oc=oc: e.add(
                              x1[:, oc * (T + 2) + T + 1: oc * (T + 2) + T + 2],
                              ps_bcB[:, oc:oc + 1], csb["w1b"][:, oc:oc + 1]),
                          reads=[bufof("psB0"), bufof("consts")], writes=[R1("x1r", oc)],
                          note=tb + f"exev{oc}")
                elif stage >= 1:
                    S.add("vector", lambda e, oc=oc: e.memset(
                              x1[:, oc * (T + 2) + T + 1: oc * (T + 2) + T + 2], 0.0),
                          writes=[R1("x1r", oc)], note=tb + f"exz{oc}")
            if stage < 2:
                continue
            for dc in (2 * p, 2 * p + 1):
                base = dc * (T + 2)
                S.add("vector", lambda e, base=base, dc=dc: e.tensor_scalar_mul(
                          bS[dc % 2][:, :], x1[:, base:base + T], csb["convw"][:, 3 * dc:3 * dc + 1]),
                      reads=[R1("x1r", dc), bufof("consts")], writes=[bufof(f"bS{dc % 2}")],
                      note=tb + f"conv{dc}a")
            for dc in (2 * p, 2 * p + 1):
                base = dc * (T + 2)
                S.add("vector", lambda e, base=base, dc=dc: e.scalar_tensor_tensor(
                          bS[dc % 2][:, :], x1[:, base + 1:base + 1 + T],
                          csb["convw"][:, 3 * dc + 1:3 * dc + 2], bS[dc % 2][:, :], ALU.mult, ALU.add),
                      reads=[R1("x1r", dc), bufof("consts"), bufof(f"bS{dc % 2}")],
                      writes=[bufof(f"bS{dc % 2}")], note=tb + f"conv{dc}b")
            for dc in (2 * p, 2 * p + 1):
                base = dc * (T + 2)
                S.add("vector", lambda e, base=base, dc=dc: e.scalar_tensor_tensor(
                          bS[dc % 2][:, :], x1[:, base + 2:base + 2 + T],
                          csb["convw"][:, 3 * dc + 2:3 * dc + 3], bS[dc % 2][:, :], ALU.mult, ALU.add),
                      reads=[R1("x1r", dc), bufof("consts"), bufof(f"bS{dc % 2}")],
                      writes=[bufof(f"bS{dc % 2}")], note=tb + f"conv{dc}c")
            for dc in (2 * p, 2 * p + 1):
                S.add("scalar", lambda e, dc=dc: e.activation(out=fm(u_bf, dc), in_=bS[dc % 2][:, :],
                                                              func=AF.Silu),
                      reads=[bufof(f"bS{dc % 2}")], writes=[R1("u", dc)], note=tb + f"usilu{dc}")
        if stage < 2:
            continue
        S.add("vector", lambda e: e.tensor_copy(
                  x1b[:, :].rearrange("p (dc c) -> p dc c", c=1),
                  x1[:, :].rearrange("p (dc c) -> p dc c", dc=NDC)[:, :, T:T + 1]),
              reads=RX1, writes=[bufof("x1b")], note=tb + "bsave")

        if stage < 3:
            continue
        for oc in range(NDC):
            j = (oc // 4) % 2
            if oc % 4 == 0:
                wslab4("wdtm", oc // 4, j, tb + f"wdtm{oc}")
            pj = oc % 3
            S.add("tensor", lambda e, j=j, oc=oc, pj=pj: mm_group(e, j, (oc % 4) * D, pj, u_bf),
                  reads=[bufof(f"wffn{j}")] + RU, writes=[bufof(f"ps{pj}")],
                  note=tb + f"dtmm{oc}")
            S.add("scalar", lambda e, pj=pj, oc=oc: e.activation(
                      out=s_scr[0][:, :], in_=ps_mm[pj][:, :], func=AF.Abs,
                      bias=csb["bdt"][:, oc:oc + 1], scale=csb["onesf"][:, 0:1]),
                  reads=[bufof(f"ps{pj}"), bufof("consts")], writes=[bufof("s_scr0")],
                  note=tb + f"sp{oc}abs")
            S.add("scalar", lambda e: e.activation(
                      out=s_scr[0][:, :], in_=s_scr[0][:, :], func=AF.Exp, scale=-1.0),
                  reads=[bufof("s_scr0")], writes=[bufof("s_scr0")], note=tb + f"sp{oc}exp")
            S.add("scalar", lambda e: e.activation(
                      out=s_scr[0][:, :], in_=s_scr[0][:, :], func=AF.Ln, bias=1.0),
                  reads=[bufof("s_scr0")], writes=[bufof("s_scr0")], note=tb + f"sp{oc}ln")
            S.add("scalar", lambda e, pj=pj, oc=oc: e.activation(
                      out=aS[0][:, :], in_=ps_mm[pj][:, :], func=AF.Relu,
                      bias=csb["bdt"][:, oc:oc + 1], scale=csb["onesf"][:, 0:1]),
                  reads=[bufof(f"ps{pj}"), bufof("consts")], writes=[bufof("aS0")],
                  note=tb + f"sp{oc}relu")
            S.add("vector", lambda e, oc=oc: e.tensor_add(fm(dt_bf, oc), s_scr[0][:, :], aS[0][:, :]),
                  reads=[bufof("s_scr0"), bufof("aS0")], writes=[R1("dt", oc)],
                  note=tb + f"dtadd{oc}")

        if stage < 4:
            continue
        def btmm(e, wsb, ps):
            last = None
            for dc in range(NDC):
                last = e.matmul(ps[0:16, :], wsb[:, dc * NST:(dc + 1) * NST], fm(u_bf, dc),
                                start=(dc == 0), stop=(dc == NDC - 1))
            return last
        S.add("tensor", lambda e: btmm(e, WBb, ps_bcB),
              reads=[bufof("WBb")] + RU, writes=[bufof("psB0")], note=tb + "btmm")
        S.add("scalar", lambda e: e.copy(Bt_sb[:, :], ps_bcB[0:16, :]),
              reads=[bufof("psB0")], writes=[bufof("Bt")], note=tb + "btev")

        S.add("tensor", lambda e: btmm(e, WCb, ps_bcC),
              reads=[bufof("WCb")] + RU, writes=[bufof("psC0")], note=tb + "ctmm")
        S.add("scalar", lambda e: e.copy(Ct_sb[:, :], ps_bcC[0:16, :]),
              reads=[bufof("psC0")], writes=[bufof("Ct")], note=tb + "ctev")


        for oc in range(NDC):
            j = (oc // 4) % 2
            if oc % 4 == 0:
                wslab4("v1m", oc // 4, j, tb + f"v1m{oc}")
            pj = oc % 3
            S.add("tensor", lambda e, j=j, oc=oc, pj=pj: mm_group(e, j, (oc % 4) * D, pj, xm_fm),
                  reads=[bufof(f"wffn{j}")] + RXM, writes=[bufof(f"ps{pj}")],
                  note=tb + f"vmm{oc}")
            S.add("scalar", lambda e, pj=pj, oc=oc: e.activation(
                      out=x1[:, oc * (T + 2) + 1: oc * (T + 2) + 1 + T], in_=ps_mm[pj][:, :],
                      func=AF.Silu, bias=csb["v1b"][:, oc:oc + 1], scale=csb["onesf"][:, 0:1]),
                  reads=[bufof(f"ps{pj}"), bufof("consts")], writes=[R1("x1r", oc)],
                  note=tb + f"vev{oc}")

        for dc in range(NDC):
            S.add("vector", lambda e, dc=dc: e.tensor_mul(fm(w_all, dc), fm(dt_bf, dc), fm(u_bf, dc)),
                  reads=[R1("dt", dc), R1("u", dc)], writes=[R1("wal", dc)], note=tb + f"w{dc}")
        S.add("vector", lambda e: e.memset(y_f[:, :], 0.0), writes=RY, note=tb + "y0")

        if stage < 5:
            continue
        ffn_steps = []

        def add_w12(hoc):
            j = (hoc // 4) % 2
            jj = 1 - j
            pj = (2 * hoc) % 3
            pj2 = (2 * hoc + 1) % 3
            s4 = hoc % 4
            if s4 == 0:
                wslab4("w1f", hoc // 4, j, tb + f"w1f{hoc}")
                wslab4("w2f", hoc // 4, jj, tb + f"w2f{hoc}")
            S.add("tensor", lambda e, j=j, s4=s4, pj=pj: mm_group(e, j, s4 * D, pj, xn_fm),
                  reads=[bufof(f"wffn{j}"), bufof("xn_fm")], writes=[bufof(f"ps{pj}")],
                  note=tb + f"w1mm{hoc}")
            S.add("scalar", lambda e, pj=pj, hoc=hoc: e.activation(
                      out=a_scr[hoc % 2][:, :], in_=ps_mm[pj][:, :], func=AF.Silu,
                      scale=csb["w1s"][:, hoc:hoc + 1], bias=csb["zerof"][:, 0:1]),
                  reads=[bufof(f"ps{pj}"), bufof("consts")], writes=[bufof(f"a_scr{hoc % 2}")],
                  note=tb + f"w1ev{hoc}")
            S.add("tensor", lambda e, jj=jj, s4=s4, pj2=pj2: mm_group(e, jj, s4 * D, pj2, xn_fm),
                  reads=[bufof(f"wffn{jj}"), bufof("xn_fm")], writes=[bufof(f"ps{pj2}")],
                  note=tb + f"w2mm{hoc}")
            S.add("vector", lambda e, pj2=pj2, hoc=hoc: e.scalar_tensor_tensor(
                      hA[:, hoc * T:(hoc + 1) * T], ps_mm[pj2][:, :],
                      csb["w2s"][:, hoc:hoc + 1], a_scr[hoc % 2][:, :], ALU.mult, ALU.mult),
                  reads=[bufof(f"ps{pj2}"), bufof(f"a_scr{hoc % 2}"), bufof("consts")],
                  writes=[R1("hA", hoc)], note=tb + f"w2ev{hoc}")

        def add_h(hoc, key, srcb, dstb, scol, note):
            j = hoc % 2
            pj = hoc % 3
            wslab(key, hoc, wffn, j, tb + f"{key}{hoc}")
            def hmm(e, j=j, pj=pj, srcb=srcb):
                last = None
                for k in range(NOC):
                    last = e.matmul(ps_mm[pj][:, :], wffn[j][:, k * 128:(k + 1) * 128],
                                    srcb[:, k * T:(k + 1) * T], start=(k == 0), stop=(k == NOC - 1))
                return last
            S.add("tensor", hmm, reads=[bufof(f"wffn{j}")] + (RHA if srcb is hA else RHB),
                  writes=[bufof(f"ps{pj}")], note=tb + f"{note}mm{hoc}")
            S.add("scalar", lambda e, pj=pj, hoc=hoc, scol=scol, dstb=dstb: e.activation(
                      out=dstb[:, hoc * T:(hoc + 1) * T], in_=ps_mm[pj][:, :], func=AF.Gelu,
                      scale=csb[scol][:, hoc:hoc + 1], bias=csb["zerof"][:, 0:1]),
                  reads=[bufof(f"ps{pj}"), bufof("consts")],
                  writes=[R1("hA" if dstb is hA else "hB", hoc)], note=tb + f"{note}ev{hoc}")

        def add_w3(oc3):
            j = oc3 % 2
            pj = oc3 % 3
            wslab("w3f", oc3, wffn, j, tb + f"w3f{oc3}")
            def w3mm(e, j=j, pj=pj):
                last = None
                for k in range(NOC):
                    last = e.matmul(ps_mm[pj][:, :], wffn[j][:, k * 128:(k + 1) * 128],
                                    hA[:, k * T:(k + 1) * T], start=(k == 0), stop=(k == NOC - 1))
                return last
            S.add("tensor", w3mm, reads=[bufof(f"wffn{j}")] + RHA,
                  writes=[bufof(f"ps{pj}")], note=tb + f"w3mm{oc3}")
            S.add("vector", lambda e, pj=pj, oc3=oc3: e.tensor_scalar_mul(
                      fm(attn, oc3), ps_mm[pj][:, :], csb["w3s"][:, oc3:oc3 + 1]),
                  reads=[bufof(f"ps{pj}"), bufof("consts")], writes=[R1("at", oc3)],
                  note=tb + f"w3ev{oc3}")

        for hoc in range(NOC):
            ffn_steps.append(lambda hoc=hoc: add_w12(hoc))
        for hoc in range(NOC):
            ffn_steps.append(lambda hoc=hoc: add_h(hoc, "h1f", hA, hB, "h1s", "h1"))
        for hoc in range(NOC):
            ffn_steps.append(lambda hoc=hoc: add_h(hoc, "h2f", hB, hA, "h2s", "h2"))
        for oc3 in range(NDC):
            ffn_steps.append(lambda oc3=oc3: add_w3(oc3))

        def add_bcast(n):
            jn = n % 2
            S.add("tensor", lambda e, n=n, jn=jn: e.matmul(
                      ps_bcBd[jn][:, :], csb["idb"][0:16, n:n + 1].broadcast_to([16, 128]),
                      Bt_sb[:, :], start=True, stop=True),
                  reads=[bufof("Bt"), bufof("consts")], writes=[bufof(f"psB{jn}")], note=tb + f"bcB{n}")
            S.add("tensor", lambda e, n=n, jn=jn: e.matmul(
                      ps_bcCd[jn][:, :], csb["idb"][0:16, n:n + 1].broadcast_to([16, 128]),
                      Ct_sb[:, :], start=True, stop=True),
                  reads=[bufof("Ct"), bufof("consts")], writes=[bufof(f"psC{jn}")], note=tb + f"bcC{n}")

        def add_pair(n, p):
            jn = n % 2
            if True:
                dcs = (2 * p, 2 * p + 1)
                for dc in dcs:
                    ja = (n * NDC + dc) % 4
                    S.add("scalar", lambda e, n=n, dc=dc, ja=ja: e.activation(
                              out=aS[ja][:, :], in_=fm(dt_bf, dc), func=AF.Exp,
                              scale=A_sb[:, dc * NST + n:dc * NST + n + 1], bias=csb["zerof"][:, 0:1]),
                          reads=[R1("dt", dc), bufof("A")], writes=[bufof(f"aS{ja}")],
                          note=tb + f"dA{n}.{dc}")
                for dc in dcs:
                    S.add("vector", lambda e, dc=dc, jn=jn: e.tensor_mul(
                              bS[dc % 2][:, :], fm(w_all, dc), ps_bcBd[jn][:, :]),
                          reads=[R1("wal", dc), bufof(f"psB{jn}")], writes=[bufof(f"bS{dc % 2}")],
                          note=tb + f"bmul{n}.{dc}")
                for dc in dcs:
                    ja = (n * NDC + dc) % 4
                    S.add("vector", lambda e, n=n, dc=dc, ja=ja: e.tensor_tensor_scan(
                              hS[dc % 2][:, :], aS[ja][:, :], bS[dc % 2][:, :],
                              hstate[:, dc * NST + n:dc * NST + n + 1], ALU.mult, ALU.add),
                          reads=[bufof(f"aS{ja}"), bufof(f"bS{dc % 2}"), R1("hst", dc)],
                          writes=[bufof(f"hS{dc % 2}")], note=tb + f"scan{n}.{dc}")
                for dc in dcs:
                    S.add("vector", lambda e, dc=dc, jn=jn: e.tensor_mul(
                              bS[dc % 2][:, :], hS[dc % 2][:, :], ps_bcCd[jn][:, :]),
                          reads=[bufof(f"hS{dc % 2}"), bufof(f"psC{jn}")], writes=[bufof(f"bS{dc % 2}")],
                          note=tb + f"ymul{n}.{dc}")
                for dc in dcs:
                    S.add("vector", lambda e, n=n, dc=dc: e.tensor_copy(
                              hstate[:, dc * NST + n:dc * NST + n + 1], hS[dc % 2][:, T - 1:T]),
                          reads=[bufof(f"hS{dc % 2}")], writes=[R1("hst", dc)],
                          note=tb + f"hsv{n}.{dc}")
                for dc in dcs:
                    S.add("gpsimd", lambda e, dc=dc: e.tensor_add(
                              fm(y_f, dc), fm(y_f, dc), bS[dc % 2][:, :]),
                          reads=[R1("y", dc), bufof(f"bS{dc % 2}")], writes=[R1("y", dc)],
                          note=tb + f"yacc{n}.{dc}", dma=False)

        scan_units = []
        for n in range(NST):
            scan_units.append(lambda n=n: add_bcast(n))
            for p in range(NDC // 2):
                scan_units.append(lambda n=n, p=p: add_pair(n, p))
        ns, nf = len(scan_units), len(ffn_steps)
        si = fi = 0
        front_done = False
        while si < ns or fi < nf:
            if fi < nf and (si >= ns or fi * ns <= si * nf):
                ffn_steps[fi]()
                fi += 1
                if fi == 36 and not front_done and i + 1 < n_tiles:
                    emit_front(i + 1)
                    front_done = True
            else:
                scan_units[si]()
                si += 1
        if not front_done and i + 1 < n_tiles:
            emit_front(i + 1)

        if stage < 6:
            continue
        for dc in range(NDC):
            S.add("vector", lambda e, dc=dc: e.scalar_tensor_tensor(
                      fm(y_f, dc), fm(u_bf, dc), csb["dsk"][:, dc:dc + 1],
                      fm(y_f, dc), ALU.mult, ALU.add),
                  reads=[R1("u", dc), bufof("consts"), R1("y", dc)],
                  writes=[R1("y", dc)], note=tb + f"gate{dc}a")
        for dc in range(NDC):
            S.add("vector", lambda e, dc=dc: e.tensor_mul(
                      fm(u_bf, dc), fm(y_f, dc),
                      x1[:, dc * (T + 2) + 1: dc * (T + 2) + 1 + T]),
                  reads=[R1("y", dc), R1("x1r", dc)],
                  writes=[R1("u", dc)], note=tb + f"gate{dc}b")
        for oc in range(NDC):
            j = (oc // 4) % 2
            if oc % 4 == 0:
                wslab4("w2m", oc // 4, j, tb + f"w2m{oc}")
            pj = oc % 3
            S.add("tensor", lambda e, j=j, oc=oc, pj=pj: mm_group(e, j, (oc % 4) * D, pj, u_bf),
                  reads=[bufof(f"wffn{j}")] + RU, writes=[bufof(f"ps{pj}")],
                  note=tb + f"w2amm{oc}")
            S.add("vector", lambda e, pj=pj, oc=oc: e.scalar_tensor_tensor(
                      fm(attn, oc), ps_mm[pj][:, :], csb["w2b"][:, oc:oc + 1],
                      fm(attn, oc), ALU.add, ALU.add),
                  reads=[bufof(f"ps{pj}"), bufof("consts"), R1("at", oc)], writes=[R1("at", oc)],
                  note=tb + f"w2aev{oc}")

        if stage < 8:
            continue
        for tc in range(NTC):
            S.add("gpsimd",
                  lambda e, i=i, tc=tc: e.dma_start(
                      out=xtl[tc % 2][:, :],
                      in_=x_d[i * T + tc * 128: i * T + (tc + 1) * 128, :]),
                  reads=[], writes=[bufof(f"xtl{tc % 2}")], note=tb + f"xtload{tc}")
            def trS(e, tc=tc):
                last = None
                for dc in range(NDC):
                    last = e.transpose(ps_tail[:, dc * 128:(dc + 1) * 128],
                                       attn[:, dc * T + tc * 128: dc * T + (tc + 1) * 128],
                                       csb["idb"][:, :])
                return last
            S.add("tensor", trS, reads=RAT + [bufof("consts")], writes=[bufof("psT")],
                  note=tb + f"trS{tc}")
            jo = tc % 2
            S.add("vector", lambda e, tc=tc, jo=jo: e.tensor_add(
                      out_sb[jo][:, :], ps_tail[:, :], xtl[tc % 2][:, :]),
                  reads=[bufof("psT"), bufof(f"xtl{tc % 2}")], writes=[bufof(f"out_sb{jo}")],
                  note=tb + f"oadd{tc}")
            S.add("gpsimd", lambda e, i=i, tc=tc, jo=jo: e.dma_start(
                      out=out_d[i * T + tc * 128: i * T + (tc + 1) * 128, :], in_=out_sb[jo][:, :]),
                  reads=[bufof(f"out_sb{jo}")], writes=[], note=tb + f"ostore{tc}")

    dbg_map = dict(xn_fm=xn_fm, xm_fm=xm_fm, x1=x1, u_bf=u_bf, dt_bf=dt_bf,
                   y_f=y_f, attn=attn, hA=hA, hB=hB, w_all=w_all,
                   xn_tm=xn_tm, ex_st=ex_st, out_sb1=out_sb[1], x_tm=xfr[0])
    for name in debug_taps:
        buf = dbg_map[name]
        shp = list(buf.shape)
        dt = buf.dtype
        od = nc.dram_tensor("dbg_" + name, shp, dt, kind="ExternalOutput").ap()
        S.add("gpsimd", lambda e, od=od, buf=buf: e.dma_start(out=od[:, :], in_=buf[:, :]),
              reads=[bufof({"xn_fm": "xn_fm", "xm_fm": "xmf0", "x1": "x1r0", "u_bf": "u0",
                            "dt_bf": "dt0", "y_f": "y0", "attn": "at0", "hA": "hA0",
                            "hB": "hB0", "w_all": "wal0", "x_tm": "xfr0", "xn_tm": "xn_tm",
                            "ex_st": "ex_st", "out_sb1": "out_sb1"}[name])],
              note=f"dbg {name}")

    S.finalize()
    sems = {}
    semctx = []
    semnames = list(ENGINES) + [e + str(j) for e, k in _DMA_RING.items() for j in range(k)]
    for e in semnames:
        h = nc.semaphore("sem_" + e)
        semctx.append(h)
        sems[e] = h.__enter__()
    blk = nc.Block()
    block = blk.__enter__()
    for ename in ENGINES:
        def body(eng, ename=ename):
            S.emit_engine(ename, eng, sems)
            if ename in _DMA_RING:
                for j in range(_DMA_RING[ename]):
                    nm = ename + str(j)
                    if S.totals.get(nm):
                        eng.wait_ge(sems[nm], S.totals[nm])
        getattr(block, ename)(body)
    blk.__exit__(None, None, None)
    for h in reversed(semctx):
        h.__exit__(None, None, None)
    for h in reversed(ctx):
        h.__exit__(None, None, None)
    return nc


_CACHE = {}


def _get_program():
    if "nc" not in _CACHE:
        _CACHE["nc"] = build_program()
    return _CACHE["nc"]


def kernel(**inputs):
    nc = _get_program()
    hp = host_prep(inputs)
    x = np.asarray(inputs["L0_attn_x"], np.float32)
    in_maps = []
    for b in range(8):
        m = dict(hp)
        m["x"] = np.ascontiguousarray(x[b])
        in_maps.append(m)
    res = run_bass_kernel_spmd(nc, in_maps, core_ids=list(range(8)))
    return np.stack([r["out"] for r in res.results], axis=0)
